# version 18
# baseline (speedup 1.0000x reference)
import contextlib
import numpy as np

import concourse.bass as bass
import concourse.mybir as mybir
from concourse.bass_utils import run_bass_kernel_spmd

F32 = mybir.dt.float32
BF16 = mybir.dt.bfloat16
U16 = mybir.dt.uint16
I32 = mybir.dt.int32
F16 = mybir.dt.float16
AF = mybir.ActivationFunctionType
ALU = mybir.AluOpType
AX = mybir.AxisListType.X

NCORES = 8
SEQ = 2
S = 2048
H = 2048
I = 5632
I2 = 2 * I
T = SEQ * S
NT = T // 128
GTILES = 4
NGRP = NT // GTILES
GUPW = H // 8
DNPW = I // 8
NEWTON = 2
QK_SCALE = 1.0 / float(np.sqrt(2048.0))
EPS_LN = 1e-5
EPS_RMS = 1e-6
MAGIC = 0x5F3759DF



SERIAL = False
FULL_EDGES = False
FULL_EDGE_ENGINES = ("vector", "gpsimd")


class Sched:
    ENG = ("tensor", "vector", "scalar", "sync", "gpsimd")

    def __init__(self):
        self.ops = []
        self.last_w = {}
        self.readers = {}
        self.barrier_op = None

    def op(self, eng, fn, reads=(), writes=(), dma=False, small=False, tag=""):
        i = len(self.ops)
        deps = set()
        for t in reads:
            j = self.last_w.get(t)
            if j is not None:
                deps.add(j)
        for t in writes:
            j = self.last_w.get(t)
            if j is not None:
                deps.add(j)
            for j2 in self.readers.get(t, ()):
                deps.add(j2)
        if self.barrier_op is not None:
            deps.add(self.barrier_op)
        if SERIAL and i > 0:
            deps.add(i - 1)
        deps.discard(i)
        o = dict(i=i, eng=eng, fn=fn, deps=sorted(deps), dma=dma, small=small,
                 inc=bool(dma), val=None, tag=tag)
        self.ops.append(o)
        for t in reads:
            self.readers.setdefault(t, []).append(i)
        for t in writes:
            self.last_w[t] = i
            self.readers[t] = []
        return i

    def barrier(self):
        last = {}
        for o in self.ops:
            last[o["eng"]] = o["i"]
        deps = sorted(last.values())
        i = len(self.ops)
        o = dict(i=i, eng="vector", fn=lambda e, g: e.nop(), deps=deps,
                 dma=False, small=False, inc=False, val=None, tag="barrier")
        self.ops.append(o)
        self.barrier_op = i

    def finalize(self):
        for o in self.ops:
            for j in o["deps"]:
                p = self.ops[j]
                if p["inc"]:
                    continue
                if p["dma"] or p["eng"] != o["eng"]:
                    p["inc"] = True
                elif p["eng"] in ("vector", "gpsimd") and p["small"]:
                    p["inc"] = True
        cnt = {}
        for o in self.ops:
            if o["inc"]:
                key = (o["eng"], o["dma"])
                step = 16 if o["dma"] else 1
                cnt[key] = cnt.get(key, 0) + step
                o["val"] = cnt[key]
        self.final_counts = cnt

    def needed_waits(self, o):
        ws = []
        for j in o["deps"]:
            p = self.ops[j]
            if not p["inc"]:
                continue
            same = p["eng"] == o["eng"]
            if same and not p["dma"] and not SERIAL and not FULL_EDGES \
                    and p["eng"] not in FULL_EDGE_ENGINES:
                if not (p["eng"] in ("vector", "gpsimd") and p["small"]):
                    continue
            ws.append((p["eng"], p["dma"], p["val"]))
        return ws


def build_core_program(dbg=(), phases=("P0", "F1", "A", "A4", "F2")):
    nc = bass.Bass("TRN2", target_bir_lowering=False)
    kind = lambda name: "ExternalOutput" if name in dbg else "Internal"

    x_in = nc.dram_tensor("x", [SEQ, S, H], F32, kind="ExternalInput")
    qkv = nc.dram_tensor("qkv", [H, 3 * H], F32, kind="ExternalInput")
    wo = nc.dram_tensor("wo", [H, H], F32, kind="ExternalInput")
    gamma = nc.dram_tensor("gamma", [H], F32, kind="ExternalInput")
    beta = nc.dram_tensor("beta", [H], F32, kind="ExternalInput")
    gup = nc.dram_tensor("gup", [I2, GUPW // 8], I32, kind="ExternalInput")
    gus = nc.dram_tensor("gus", [2, I2], F16, kind="ExternalInput")
    dnp = nc.dram_tensor("dnp", [H // 8, DNPW], I32, kind="ExternalInput")
    dns = nc.dram_tensor("dns", [44, H // 8], F16, kind="ExternalInput")
    out = nc.dram_tensor("out", [SEQ, S, H], F32, kind="ExternalOutput")

    wguT = nc.dram_tensor("wguT", [H, I2], BF16, kind="Internal")
    wdnT8 = nc.dram_tensor("wdnT8", [8, I, H // 8], BF16, kind="Internal")
    locg = nc.dram_tensor("locg", [H // 8, I2], BF16, kind="Internal")
    locd = nc.dram_tensor("locd", [I, H // 8], BF16, kind="Internal")
    x1s = nc.dram_tensor("x1s", [NT, 128, H], F32, kind=kind("x1s"))
    x2s = nc.dram_tensor("x2s", [NT, 128, H], F32, kind=kind("x2s"))
    qscr = nc.dram_tensor("qscr", [SEQ, 16, 128, S], BF16, kind=kind("qscr"))
    kscr = nc.dram_tensor("kscr", [SEQ, 16, 128, S], BF16, kind=kind("kscr"))
    vscr = nc.dram_tensor("vscr", [SEQ, S, H], BF16, kind=kind("vscr"))
    avscr = nc.dram_tensor("avscr", [SEQ, 16, 128, S], BF16, kind=kind("avscr"))
    xnscr = None
    if "xnscr" in dbg:
        xnscr = nc.dram_tensor("xnscr", [SEQ, 16, 128, S], BF16, kind="ExternalOutput")
    if "wgdbg" in dbg:
        wgdbg = nc.dram_tensor("wgdbg", [H, I2], BF16, kind="ExternalOutput")
        wddbg = nc.dram_tensor("wddbg", [8, I, H // 8], BF16, kind="ExternalOutput")
    rsumscr = None
    if "rsumscr" in dbg:
        rsumscr = nc.dram_tensor("rsumscr", [2, 128, 32], F32, kind="ExternalOutput")

    sch = Sched()
    st = contextlib.ExitStack()
    en = st.enter_context

    iot = en(nc.sbuf_tensor("iot", [128, 128], I32))
    idb = en(nc.sbuf_tensor("idb", [128, 128], BF16))
    idf = en(nc.sbuf_tensor("idf", [128, 128], F32))
    maskc = en(nc.sbuf_tensor("maskc", [128, 128], F32))
    rsum = en(nc.sbuf_tensor("rsum", [128, 32], F32))
    mxp = en(nc.sbuf_tensor("mxp", [128, 4], F32))
    sc16g = en(nc.sbuf_tensor("sc16g", [2, 512], F16))
    sc32g = en(nc.sbuf_tensor("sc32g", [2, 512], F32))
    sc16d = en(nc.sbuf_tensor("sc16d", [44, 128], F16))
    sc32d = en(nc.sbuf_tensor("sc32d", [44, 128], F32))
    scTg = [en(nc.sbuf_tensor(f"scTg{p}", [128, 16], F32)) for p in range(2)]
    scTd = [en(nc.sbuf_tensor(f"scTd{p}", [128, 44], F32)) for p in range(2)]

    class StatSet:
        def __init__(self, p):
            self.bns = en(nc.sbuf_tensor(f"bns{p}", [128, 24], F32))
            self.mv = en(nc.sbuf_tensor(f"mv{p}", [128, 2], F32))
            self.vp = en(nc.sbuf_tensor(f"vp{p}", [128, 1], F32))
            self.t1 = en(nc.sbuf_tensor(f"t1_{p}", [128, 1], I32))
            self.r = en(nc.sbuf_tensor(f"r{p}", [128, 1], F32))
            self.hc = en(nc.sbuf_tensor(f"hc{p}", [128, 1], F32))
            self.nmr = en(nc.sbuf_tensor(f"nmr{p}", [128, 1], F32))
            self.tok = f"stats{p}"
    stats = [StatSet(p) for p in range(4)]
    mxs = [en(nc.sbuf_tensor(f"mx{p}", [128, 1], F32)) for p in range(2)]
    nbias = [en(nc.sbuf_tensor(f"nb{p}", [128, 1], F32)) for p in range(2)]
    rcps = [en(nc.sbuf_tensor(f"rcp{p}", [128, 1], F32)) for p in range(2)]

    pall = en(nc.psum_tensor("pall", [128, 4096], F32))
    PB = lambda k: pall[:, 512 * k:512 * (k + 1)]
    PBb = lambda k: pall[:, 512 * k:512 * (k + 1)].bitcast(BF16)
    PTOK = lambda k: f"P{k}"

    ACOLS = 51264
    arena_t = en(nc.sbuf_tensor("arena", [128, ACOLS], F32))

    class Arena:
        def __init__(self):
            self.base = 0
            self.stack = []
        def push(self):
            self.stack.append(self.base)
        def pop(self):
            self.base = self.stack.pop()
        def f32(self, n):
            o = self.base
            self.base += n
            assert self.base <= ACOLS, f"arena overflow {self.base}"
            return arena_t[:, o:o + n]
        def bf16(self, n):
            assert n % 2 == 0
            return self.f32(n // 2).bitcast(BF16)
        def u16(self, n):
            assert n % 2 == 0
            return self.f32(n // 2).bitcast(U16)
        def i32(self, n):
            return self.f32(n).bitcast(I32)
    A = Arena()

    def f_iota(e, g):
        return e.iota(iot[:], [[-1, 128]], base=0, channel_multiplier=1)
    sch.op("gpsimd", f_iota, writes=["iot"], tag="iota")

    def f_ids(e, g):
        e.tensor_scalar(idf[:], iot[:], 0, 1.0, ALU.is_equal, ALU.mult)
        e.tensor_scalar(idb[:], iot[:], 0, 1.0, ALU.is_equal, ALU.mult)
        return e.tensor_scalar(maskc[:], iot[:], 0, -1e30, ALU.is_lt, ALU.mult)
    sch.op("vector", f_ids, reads=["iot"], writes=["idf", "idb", "maskc"], tag="ids")

    def newton_rsqrt(v, g, stt, src_ap, scale, bias):
        ins = v.tensor_scalar(stt.vp[:], src_ap, scale, bias, ALU.mult, ALU.add)
        g(ins)
        ins = v.tensor_scalar(stt.t1[:], stt.vp[:].bitcast(I32), 1, None,
                              ALU.logical_shift_right)
        g(ins)
        ins = v.tensor_scalar(stt.t1[:], stt.t1[:], -1, MAGIC, ALU.mult, ALU.add)
        g(ins)
        for it in range(NEWTON):
            rin = stt.t1[:].bitcast(F32) if it == 0 else stt.r[:]
            ins = v.tensor_tensor(stt.hc[:], rin, rin, ALU.mult); g(ins)
            ins = v.tensor_tensor(stt.hc[:], stt.hc[:], stt.vp[:], ALU.mult); g(ins)
            ins = v.tensor_scalar(stt.hc[:], stt.hc[:], -0.5, 1.5, ALU.mult, ALU.add); g(ins)
            ins = v.tensor_tensor(stt.r[:], rin, stt.hc[:], ALU.mult)
            if it < NEWTON - 1:
                g(ins)
        return ins

    def ln_stats_fn(xap, stt, eps):
        def f(v, g):
            for c in range(4):
                ins = v.bn_stats(stt.bns[:, 6 * c:6 * (c + 1)], xap[:, 512 * c:512 * (c + 1)])
            g(ins)
            ins = v.bn_aggr(stt.mv[:], stt.bns[:]); g(ins)
            ins = newton_rsqrt(v, g, stt, stt.mv[:, 1:2], 1.0, eps)
            g(ins)
            return v.tensor_scalar(stt.nmr[:], stt.mv[:, 0:1], stt.r[:], -1.0,
                                   ALU.mult, ALU.mult)
        return f

    def rms_stats_fn(ssq_ap, stt, n, eps):
        def f(v, g):
            return newton_rsqrt(v, g, stt, ssq_ap, 1.0 / n, eps)
        return f

    def apply_fn(dst, src, stt):
        def f(s, g):
            return s.activation(dst, src, AF.Identity, bias=stt.nmr[:], scale=stt.r[:])
        return f

    def dma_fn(e_dst, e_src):
        def f(e, g):
            return e.dma_start(e_dst, e_src)
        return f

    def transpose8_fn(pbank_bf, src, n, ident, width=128):
        def f(t, g):
            ins = None
            for j in range(n):
                ins = t.transpose(pbank_bf[:, j * width:(j + 1) * width],
                                  src[:, j * width:(j + 1) * width], ident)
            return ins
        return f

    def copy_fn(dst, src):
        def f(e, g):
            return e.tensor_copy(dst, src)
        return f

    def scopy_fn(dst, src):
        def f(e, g):
            return e.copy(dst, src)
        return f

    A.push()
    pk4 = [A.i32(8 * 32) for _ in range(2)]
    pkd = [A.i32(DNPW) for _ in range(2)]
    tsetV = (A.u16(2 * DNPW), A.u16(2 * DNPW), A.bf16(2 * DNPW),
             A.bf16(2 * DNPW), A.u16(2 * DNPW))
    valsg = A.bf16(8 * 256)
    svalg = [A.bf16(8 * 256) for _ in range(2)]
    valsd = A.bf16(I)
    svald = [A.bf16(I) for _ in range(2)]
    stagg = [A.bf16(2 * 1024) for _ in range(2)]
    stagd = A.bf16(44 * 256)

    def decode_fn(pk, nwords, vals_ap, ks, tset):
        nh = 2 * nwords
        pku = pk.bitcast(U16)
        va = vals_ap.bitcast(U16).rearrange("p (j k) -> p j k", k=4)
        ta, tg, tt2, tfm, tsb = tset

        def f(v, g):
            ins = None
            for k in ks:
                a = ta[:, :nh]
                if 4 * k <= 6:
                    v.tensor_scalar(a, pku, 6 - 4 * k, 0x01C0,
                                    ALU.logical_shift_left, ALU.bitwise_and)
                else:
                    v.tensor_scalar(a, pku, 4 * k - 6, 0x01C0,
                                    ALU.logical_shift_right, ALU.bitwise_and)
                gg = tg[:, :nh]
                v.tensor_scalar(gg, a, 0x3F00, None, ALU.add)
                gbf = gg.bitcast(BF16)
                v.tensor_scalar(tt2[:, :nh], gbf, 2.0, -1.0, ALU.mult, ALU.add)
                v.tensor_tensor(tfm[:, :nh], gbf, tt2[:, :nh], ALU.min)
                v.tensor_scalar(tsb[:, :nh], pku, 1 << (4 * k + 3), 12 - 4 * k,
                                ALU.bitwise_and, ALU.logical_shift_left)
                ins = v.tensor_tensor(va[:, :, k], tfm[:, :nh].bitcast(U16),
                                      tsb[:, :nh], ALU.bitwise_or)
            return ins
        return f

    def scale_fn(pairs, scT_ap):
        def f(v, g):
            ins = None
            for dst, s_ap, gi in pairs:
                ins = v.tensor_scalar(dst, s_ap, scT_ap[:, gi:gi + 1], None, ALU.mult)
            return ins
        return f

    def p0_emit():
        gup_r = gup.rearrange("(t p) w -> p t w", p=128)
        for g4 in range(11):
            pk = pk4[g4 % 2]
            pktok = f"pk{g4 % 2}"
            pk3 = pk.rearrange("p (t w) -> p t w", t=8)
            sch.op("sync", dma_fn(pk3, gup_r[:, 8 * g4:8 * (g4 + 1), :]),
                   writes=[pktok], dma=True, tag="pkl")
            for hh in range(2):
                sch.op("sync", dma_fn(sc16g[:, :],
                                      gus[:, 1024 * g4 + 512 * hh:1024 * g4 + 512 * (hh + 1)]),
                       reads=["sc32g"], writes=["sc16g"], dma=True, tag="scl")
                sch.op("vector", copy_fn(sc32g[:, :], sc16g[:, :]),
                       reads=["sc16g"], writes=["sc32g"], tag="scc")

                def sctr(hh=hh):
                    def f(t, g):
                        ins = None
                        for j in range(4):
                            ins = t.transpose(PB(6)[:, 8 * hh + 2 * j:8 * hh + 2 * j + 2],
                                              sc32g[:, 128 * j:128 * (j + 1)], idf[:2, :2])
                        return ins
                    return f
                sch.op("tensor", sctr(), reads=["sc32g", "idf"], writes=[PTOK(6)], tag="sctr")
                sch.op("scalar", scopy_fn(scTg[g4 % 2][:, 8 * hh:8 * (hh + 1)],
                                          PB(6)[:, 8 * hh:8 * (hh + 1)]),
                       reads=[PTOK(6)], writes=[f"scT{g4 % 2}"], tag="sce")
            sch.op("vector", decode_fn(pk, 8 * 32, valsg[:, :], (0, 1, 2, 3), tsetV),
                   reads=[pktok], writes=["valsV"], tag="dec")
            sva = svalg[g4 % 2]
            v3 = valsg.rearrange("p (t h) -> p t h", t=8)
            s3 = sva.rearrange("p (t h) -> p t h", t=8)
            pairs = [(s3[:, j, 128 * gi:128 * (gi + 1)], v3[:, j, 128 * gi:128 * (gi + 1)],
                      2 * j + gi) for j in range(8) for gi in range(2)]
            sch.op("vector", scale_fn(pairs, scTg[g4 % 2][:]),
                   reads=["valsV", f"scT{g4 % 2}"], writes=[f"sval{g4 % 2}"], tag="scale")
            stg = stagg[g4 % 2]
            stg3 = stg.rearrange("p (c o) -> p c o", c=2)
            for half in range(2):
                bank = 6 + half

                def wtr(half=half, bank=bank, s3=s3):
                    def f(t, g):
                        ins = None
                        for j in range(8):
                            ins = t.transpose(
                                PBb(bank)[:, 128 * j:128 * (j + 1)],
                                s3[:, j, 128 * half:128 * (half + 1)], idb[:])
                        return ins
                    return f
                sch.op("tensor", wtr(), reads=[f"sval{g4 % 2}", "idb"],
                       writes=[PTOK(bank)], tag="wtr")

                def wte(half=half, bank=bank, stg3=stg3):
                    def f(s, g):
                        return s.copy(stg3[:, half, :], PBb(bank)[:, 0:1024])
                    return f
                sch.op("scalar", wte(), reads=[PTOK(bank)],
                       writes=[f"stagg{g4 % 2}"], tag="wte")
            for c in range(2):
                dsl = locg[c * 128:(c + 1) * 128, 1024 * g4:1024 * (g4 + 1)]
                sch.op("gpsimd", dma_fn(dsl, stg3[:, c, :]), reads=[f"stagg{g4 % 2}"],
                       writes=[f"locg_{g4}_{c}"], dma=True, tag="wst")
        def ccg(e, g):
            return e.collective_compute(
                "AllGather", mybir.AluOpType.bypass,
                replica_groups=[list(range(NCORES))],
                ins=[locg.ap().opt()], outs=[wguT.ap().opt()])
        sch.op("gpsimd", ccg,
               reads=[f"locg_{g4}_{c}" for g4 in range(11) for c in range(2)],
               writes=["wguT_cc"], tag="ccg")

        dnp_r = dnp.rearrange("(t p) w -> p t w", p=128)
        stagd3 = stagd.rearrange("p (c o) -> p c o", c=44)
        for dt_ in range(2):
            pk = pkd[dt_ % 2]
            pktok = f"pkd{dt_ % 2}"
            sch.op("sync", dma_fn(pk[:, :], dnp_r[:, dt_, :]),
                   writes=[pktok], dma=True, tag="pkdl")
            sch.op("sync", dma_fn(sc16d[:, :], dns[:, 128 * dt_:128 * (dt_ + 1)]),
                   reads=["sc32d"], writes=["sc16d"], dma=True, tag="sdl")
            sch.op("vector", copy_fn(sc32d[:, :], sc16d[:, :]),
                   reads=["sc16d"], writes=["sc32d"], tag="sdc")

            def sdtr(dt_=dt_):
                def f(t, g):
                    return t.transpose(PB(6)[:, 0:44], sc32d[:, :], idf[:44, :44])
                return f
            sch.op("tensor", sdtr(), reads=["sc32d", "idf"], writes=[PTOK(6)], tag="sdtr")
            sch.op("scalar", scopy_fn(scTd[dt_ % 2][:, :], PB(6)[:, 0:44]),
                   reads=[PTOK(6)], writes=[f"scTd{dt_ % 2}"], tag="sde")
            sch.op("vector", decode_fn(pk[:, :], DNPW, valsd[:, :], (0, 1, 2, 3), tsetV),
                   reads=[pktok], writes=["valsV"], tag="ddec")
            sva = svald[dt_ % 2]
            pairs = [(sva[:, 128 * gi:128 * (gi + 1)], valsd[:, 128 * gi:128 * (gi + 1)], gi)
                     for gi in range(44)]
            sch.op("vector", scale_fn(pairs, scTd[dt_ % 2][:]),
                   reads=["valsV", f"scTd{dt_ % 2}"], writes=[f"svald{dt_ % 2}"], tag="dscale")
            for half in range(0, 44, 8):
                hn = min(8, 44 - half)
                bank = 6 + (half // 8) % 2
                sch.op("tensor", transpose8_fn(PBb(bank), sva[:, half * 128:], hn, idb[:]),
                       reads=[f"svald{dt_ % 2}", "idb"], writes=[PTOK(bank)], tag="dwtr")

                def dwte(half=half, hn=hn, bank=bank, dt_=dt_):
                    dsl = stagd3[:, half:half + hn, 128 * dt_:128 * (dt_ + 1)]
                    def f(s, g):
                        return s.copy(dsl, PBb(bank)[:, 0:hn * 128]
                                      .rearrange("p (c o) -> p c o", o=128))
                    return f
                sch.op("scalar", dwte(), reads=[PTOK(bank)], writes=["stagd"], tag="dwte")
        for c in range(44):
            dsl = locd[c * 128:(c + 1) * 128, :]
            sch.op("gpsimd", dma_fn(dsl, stagd3[:, c, :]), reads=["stagd"],
                   writes=[f"locd_{c}"], dma=True, tag="dst")
        def ccd(e, g):
            return e.collective_compute(
                "AllGather", mybir.AluOpType.bypass,
                replica_groups=[list(range(NCORES))],
                ins=[locd.ap().opt()],
                outs=[wdnT8.rearrange("r i o -> (r i) o").opt()])
        sch.op("gpsimd", ccd, reads=[f"locd_{c}" for c in range(44)],
               writes=["wdnT_cc"], tag="ccd")

    if "P0" in phases:
        p0_emit()
        if "wgdbg" in dbg:
            for c in range(16):
                sch.op("gpsimd", dma_fn(wgdbg[c * 128:(c + 1) * 128, :],
                                        wguT[c * 128:(c + 1) * 128, :]),
                       reads=["wguT_cc"], writes=[f"wgd{c}"], dma=True, tag="wgd")
            for rr in range(8):
                sch.op("gpsimd", dma_fn(wddbg[rr], wdnT8[rr]),
                       reads=["wdnT_cc"], writes=[f"wdd{rr}"], dma=True, tag="wdd")
        sch.barrier()
    A.pop()

    def emit_ffn(xsrc, xdst, phase):
        A.push()
        u_fm = A.bf16(16 * 512).rearrange("p (c t) -> p c t", c=16)
        h_act = A.bf16(44 * 512).rearrange("p (c t) -> p c t", c=44)
        wg = [A.bf16(16 * 256).rearrange("p (c t) -> p c t", c=16) for _ in range(2)]
        wu = [A.bf16(16 * 256).rearrange("p (c t) -> p c t", c=16) for _ in range(2)]
        wd = [A.bf16(44 * 256).rearrange("p (c t) -> p c t", c=44) for _ in range(2)]
        y_tm = A.bf16(GTILES * 2048).rearrange("p (j h) -> p j h", j=GTILES)
        yfm = [A.bf16(512) for _ in range(2)]
        xt2 = [A.f32(2048) for _ in range(2)]
        xt4 = [A.f32(2048) for _ in range(2)]
        outt = [A.f32(2048)]
        utm = A.bf16(2048)
        gsil = [A.bf16(512) for _ in range(2)]

        def stage1(grp):
            for tt in range(GTILES):
                tile = grp * GTILES + tt
                xb = xt2[tt % 2]
                xtok = f"{phase}xt{tt % 2}"
                stt = stats[tt % 2]
                sch.op("gpsimd", dma_fn(xb, xsrc[tile]), writes=[xtok], dma=True, tag="xld")
                sch.op("vector", ln_stats_fn(xb, stt, EPS_LN),
                       reads=[xtok], writes=[stt.tok], small=True, tag="lnst")
                sch.op("scalar", apply_fn(utm[:], xb, stt),
                       reads=[xtok, stt.tok], writes=[f"{phase}utm"], tag="uap")
                for half in range(2):
                    bank = 0 + half
                    sch.op("tensor",
                           transpose8_fn(PBb(bank), utm[:, half * 1024:], 8, idb[:]),
                           reads=[f"{phase}utm", "idb"], writes=[PTOK(bank)], tag="utr")
                    dsl = u_fm[:, half * 8:half * 8 + 8, tt * 128:(tt + 1) * 128]
                    sch.op("scalar", scopy_fn(dsl, PBb(bank)[:, 0:1024]),
                           reads=[PTOK(bank)], writes=["ufm"], tag="ute")

        def load2(b):
            wgb, wub = wg[b % 2], wu[b % 2]
            wgt, wut = f"wg{b % 2}", f"wu{b % 2}"
            src_g = wguT.rearrange("(c p) o -> p c o", p=128)[:, :, 256 * b:256 * (b + 1)]
            src_u = wguT.rearrange("(c p) o -> p c o", p=128)[:, :, I + 256 * b:I + 256 * (b + 1)]
            sch.op("sync", dma_fn(wgb, src_g), reads=["wguT_cc"], writes=[wgt], dma=True, tag="wgl")
            sch.op("sync", dma_fn(wub, src_u), reads=["wguT_cc"], writes=[wut], dma=True, tag="wul")

        def stage2(grp, preloaded):
            for b in range(22):
                wgb, wub = wg[b % 2], wu[b % 2]
                wgt, wut = f"wg{b % 2}", f"wu{b % 2}"
                if not (preloaded and b < 2):
                    load2(b)
                for s_ in range(2):
                    bg, bu = (0, 1) if s_ == 0 else (2, 3)
                    isl = 2 * b + s_

                    def mm32(bg=bg, bu=bu, wgb=wgb, wub=wub, s_=s_):
                        def f(t, g):
                            for c in range(16):
                                t.matmul(PB(bg), wgb[:, c, s_ * 128:(s_ + 1) * 128],
                                         u_fm[:, c, :], start=(c == 0), stop=(c == 15))
                            ins = None
                            for c in range(16):
                                ins = t.matmul(PB(bu), wub[:, c, s_ * 128:(s_ + 1) * 128],
                                               u_fm[:, c, :], start=(c == 0), stop=(c == 15))
                            return ins
                        return f
                    sch.op("tensor", mm32(), reads=[wgt, wut, "ufm"],
                           writes=[PTOK(bg), PTOK(bu)], tag="gumm")

                    def silu_fn(bg=bg, p=isl % 2):
                        def f(s, g):
                            return s.activation(gsil[p][:], PB(bg), AF.Silu)
                        return f
                    sch.op("scalar", silu_fn(), reads=[PTOK(bg)],
                           writes=[f"gs{isl % 2}"], tag="silu")

                    def hmul_fn(bu=bu, isl=isl, p=isl % 2):
                        def f(v, g):
                            return v.tensor_tensor(h_act[:, isl, :], PB(bu),
                                                   gsil[p][:], ALU.mult)
                        return f
                    sch.op("vector", hmul_fn(), reads=[PTOK(bu), f"gs{isl % 2}"],
                           writes=["hact"], tag="hmul")

        def stage3(grp):
            for hb in range(8):
                wdb = wd[hb % 2]
                wdt = f"wd{hb % 2}"
                src_d = wdnT8.rearrange("r (c p) o -> r p c o", p=128)[hb]
                sch.op("sync", dma_fn(wdb, src_d), reads=["wdnT_cc"], writes=[wdt], dma=True, tag="wdl")
                for s_ in range(2):
                    hs = 2 * hb + s_
                    bank_y = 4 + (hs % 2)

                    def dmm(wdb=wdb, s_=s_, bank_y=bank_y):
                        def f(t, g):
                            ins = None
                            for c in range(44):
                                ins = t.matmul(PB(bank_y), wdb[:, c, s_ * 128:(s_ + 1) * 128],
                                               h_act[:, c, :], start=(c == 0), stop=(c == 43))
                            return ins
                        return f
                    sch.op("tensor", dmm(), reads=[wdt, "hact"],
                           writes=[PTOK(bank_y)], tag="dmm")
                    yf = yfm[hs % 2]
                    sch.op("scalar", scopy_fn(yf[:], PB(bank_y)),
                           reads=[PTOK(bank_y)], writes=[f"yfm{hs % 2}"], tag="yev")
                    bank_t = 6 + (hs % 2)
                    sch.op("tensor", transpose8_fn(PBb(bank_t), yf[:], GTILES, idb[:]),
                           reads=[f"yfm{hs % 2}", "idb"], writes=[PTOK(bank_t)], tag="ytr")

                    def yte(hs=hs, bank_t=bank_t):
                        dsl = y_tm[:, :, hs * 128:(hs + 1) * 128]
                        def f(s, g):
                            return s.copy(dsl, PBb(bank_t)[:, 0:GTILES * 128]
                                          .rearrange("p (j h) -> p j h", j=GTILES))
                        return f
                    sch.op("scalar", yte(), reads=[PTOK(bank_t)], writes=["ytm"], tag="yte")

        def stage4(grp):
            for tt in range(GTILES):
                tile = grp * GTILES + tt
                ob = outt[0]
                otok = f"{phase}ot0"
                stt = stats[2 + tt % 2]
                xb = xt4[tt % 2]
                xtok = f"{phase}xr{tt % 2}"
                sch.op("gpsimd", dma_fn(xb, xsrc[tile]), writes=[xtok], dma=True, tag="xl2")

                def sq_fn(tt=tt, ob=ob, stt=stt):
                    def f(s, g):
                        return s.activation(ob[:], y_tm[:, tt, :], AF.Square,
                                            accum_out=stt.vp[:])
                    return f
                sch.op("scalar", sq_fn(), reads=["ytm"], writes=[otok, stt.tok], tag="ysq")
                sch.op("vector", rms_stats_fn(stt.vp[:], stt, float(H), EPS_RMS),
                       reads=[stt.tok], writes=[stt.tok], small=True, tag="rms")

                def stt_fn(tt=tt, ob=ob, stt=stt, xb=xb):
                    def f(v, g):
                        ins = v.tensor_scalar(stt.hc[:], stt.r[:], 0.5, None, ALU.mult)
                        g(ins)
                        return v.scalar_tensor_tensor(ob[:], y_tm[:, tt, :], stt.hc[:],
                                                      xb, ALU.mult, ALU.add)
                    return f
                sch.op("vector", stt_fn(), reads=["ytm", stt.tok, xtok],
                       writes=[otok, stt.tok], tag="res")
                sch.op("gpsimd", dma_fn(xdst[tile], ob[:]), reads=[otok],
                       writes=[f"{phase}st{tile}"], dma=True, tag="xst")

        stage1(0)
        for grp in range(NGRP):
            stage2(grp, grp > 0)
            if grp + 1 < NGRP:
                load2(0)
                load2(1)
            stage3(grp)
            if grp + 1 < NGRP:
                stage1(grp + 1)
            stage4(grp)
        A.pop()

    if "F1" in phases:
        emit_ffn(x_in.rearrange("b (n p) h -> (b n) p h", p=128), x1s, "F1")
        sch.barrier()

    qkv_r = qkv.rearrange("(c p) o -> p c o", p=128)
    wo_r = wo.rearrange("(c p) o -> p c o", p=128)

    for sq_ in (range(SEQ) if "A" in phases else ()):
        A.push()
        grep = A.bf16(2048)
        brep = A.bf16(2048)
        xn_fm = A.bf16(16 * 2048).rearrange("p (c t) -> p c t", c=16)
        xt = A.f32(2048)
        v_t = A.bf16(2048)
        xntm = [A.bf16(2048) for _ in range(2)]
        qwf = [A.f32(2048).rearrange("p (c t) -> p c t", c=16) for _ in range(2)]
        qwb = [A.bf16(2048).rearrange("p (c t) -> p c t", c=16) for _ in range(2)]
        vwf = A.f32(16 * 512).rearrange("p (c t) -> p c t", c=16)
        vwb = A.bf16(16 * 512).rearrange("p (c t) -> p c t", c=16)
        qst = [A.bf16(512) for _ in range(2)]

        gbf = A.f32(2048)
        def gb_bcast(t_, src):
            sch.op("sync", dma_fn(gbf, bass.AP(src, 0, [[0, 128], [1, 2048]])),
                   reads=[f"gb{id(t_)}"], writes=["gbf"], dma=True, tag="gbl")
            sch.op("vector", copy_fn(t_, gbf), reads=["gbf"],
                   writes=[f"gb{id(t_)}"], tag="gbc")
        gb_bcast(grep, gamma)
        gb_bcast(brep, beta)

        def qk_load_cast(m):
            p = m % 2
            sch.op("sync", dma_fn(qwf[p][:, :, :], qkv_r[:, :, m * 128:(m + 1) * 128]),
                   reads=[f"qwb{p}_r"], writes=[f"qwf{p}"], dma=True, tag="qwl")
            sch.op("scalar", scopy_fn(qwb[p][:, :, :], qwf[p][:, :, :]),
                   reads=[f"qwf{p}"], writes=[f"qwb{p}"], tag="qwc")
        qk_load_cast(0)
        qk_load_cast(1)

        def ln2_stats_fn(xap, stt):
            def f(v, g):
                for c in range(4):
                    ins = v.bn_stats(stt.bns[:, 6 * c:6 * (c + 1)],
                                     xap[:, 512 * c:512 * (c + 1)])
                g(ins)
                ins = v.bn_aggr(stt.mv[:], stt.bns[:]); g(ins)
                ins = newton_rsqrt(v, g, stt, stt.mv[:, 1:2],
                                   1.0 + EPS_LN, EPS_LN * EPS_LN)
                g(ins)
                return v.tensor_scalar(stt.nmr[:], stt.mv[:, 0:1], stt.r[:], -1.0,
                                       ALU.mult, ALU.mult)
            return f

        for tl in range(16):
            tile = sq_ * 16 + tl
            sch.op("gpsimd", dma_fn(xt, x1s[tile]), writes=["Axt"], dma=True, tag="axl")
            sch.op("vector", ln2_stats_fn(xt, stats[0]),
                   reads=["Axt"], writes=[stats[0].tok], small=True, tag="ast1")
            sch.op("scalar", apply_fn(v_t[:], xt, stats[0]),
                   reads=["Axt", stats[0].tok], writes=["Avt"], tag="aap1")

            def gbmul(p=tl % 2):
                def f(v, g):
                    ins = v.tensor_tensor(v_t[:], v_t[:], grep[:], ALU.mult)
                    g(ins)
                    return v.tensor_tensor(xntm[p][:], v_t[:], brep[:], ALU.add)
                return f
            sch.op("vector", gbmul(), reads=["Avt", f"gb{id(grep)}", f"gb{id(brep)}"],
                   writes=[f"Axn{tl % 2}", "Avt"], tag="agb")
            for half in range(2):
                bank = 6 + half
                sch.op("tensor", transpose8_fn(PBb(bank), xntm[tl % 2][:, half * 1024:], 8, idb[:]),
                       reads=[f"Axn{tl % 2}", "idb"], writes=[PTOK(bank)], tag="xtr")
                dsl = xn_fm[:, half * 8:half * 8 + 8, tl * 128:(tl + 1) * 128]
                sch.op("scalar", scopy_fn(dsl, PBb(bank)[:, 0:1024]),
                       reads=[PTOK(bank)], writes=[f"xnfm_q{tl // 4}"], tag="xte")

        if xnscr is not None:
            xn_dst = xnscr.rearrange("s c p t -> s p c t")[sq_]
            for tl in range(16):
                sch.op("gpsimd", dma_fn(xn_dst[:, :, tl * 128:(tl + 1) * 128],
                                        xn_fm[:, :, tl * 128:(tl + 1) * 128]),
                       reads=[f"xnfm_q{tl // 4}"], writes=[f"xnd{sq_}_{tl}"], dma=True, tag="xnd")

        for m in range(32):
            p = m % 2
            if m >= 2:
                qk_load_cast(m)
            dstt = qscr if m < 16 else kscr
            mm = m % 16
            for nb in range(4):
                bank = nb

                def qmm(p=p, nb=nb, bank=bank):
                    def f(t, g):
                        ins = None
                        for c in range(16):
                            ins = t.matmul(PB(bank), qwb[p][:, c, :],
                                           xn_fm[:, c, nb * 512:(nb + 1) * 512],
                                           start=(c == 0), stop=(c == 15))
                        return ins
                    return f
                sch.op("tensor", qmm(), reads=[f"qwb{p}", f"xnfm_q{nb}"], writes=[PTOK(bank)], tag="qmm")
                qp = qst[nb % 2]
                sch.op("scalar", scopy_fn(qp[:], PB(bank)), reads=[PTOK(bank)],
                       writes=[f"qst{nb % 2}"], tag="qev")
                sch.op("gpsimd", dma_fn(dstt[sq_, mm, :, nb * 512:(nb + 1) * 512], qp[:]),
                       reads=[f"qst{nb % 2}"], writes=[f"qs_{m}_{nb}"], dma=True, tag="qsw")
            sch.op("vector", lambda e, g: e.nop(), reads=[f"qwb{p}"],
                   writes=[f"qwb{p}_r"], small=True, tag="qwr")

        for fb in range(4):
            sch.op("sync", dma_fn(vwf[:, :, :], qkv_r[:, :, 4096 + fb * 512:4096 + (fb + 1) * 512]),
                   reads=["vwb_r"], writes=["vwf"], dma=True, tag="vwl")
            sch.op("scalar", scopy_fn(vwb[:, :, :], vwf[:, :, :]),
                   reads=["vwf"], writes=["vwb"], tag="vwc")
            for tl in range(16):
                bank = tl % 4

                def vmm(tl=tl, bank=bank):
                    def f(t, g):
                        ins = None
                        for c in range(16):
                            ins = t.matmul(PB(bank), xn_fm[:, c, tl * 128:(tl + 1) * 128],
                                           vwb[:, c, :], start=(c == 0), stop=(c == 15))
                        return ins
                    return f
                sch.op("tensor", vmm(), reads=["vwb", f"xnfm_q{tl // 4}"], writes=[PTOK(bank)], tag="vmm")
                qp = qst[tl % 2]
                sch.op("scalar", scopy_fn(qp[:], PB(bank)), reads=[PTOK(bank)],
                       writes=[f"qst{tl % 2}"], tag="vev")
                sch.op("gpsimd", dma_fn(vscr[sq_, tl * 128:(tl + 1) * 128, fb * 512:(fb + 1) * 512], qp[:]),
                       reads=[f"qst{tl % 2}"], writes=[f"vs_{fb}_{tl}"], dma=True, tag="vsw")
            sch.op("vector", lambda e, g: e.nop(), reads=["vwb"], writes=["vwb_r"],
                   small=True, tag="vwr")
        A.pop()
        sch.barrier()

        A.push()
        k_fm = A.bf16(16 * 2048).rearrange("p (c t) -> p c t", c=16)
        v_sb = A.bf16(16 * 2048).rearrange("p (c t) -> p c t", c=16)
        q_t = [A.bf16(16 * 128).rearrange("p (c t) -> p c t", c=16) for _ in range(2)]
        probs = [A.bf16(2048) for _ in range(2)]
        probsT = [A.bf16(16 * 128).rearrange("p (c t) -> p c t", c=16) for _ in range(2)]
        av_tm = [A.bf16(2048) for _ in range(2)]
        avst = [A.bf16(16 * 128).rearrange("p (c t) -> p c t", c=16) for _ in range(2)]

        kv_src = kscr.rearrange("s c p t -> s p c t")[sq_]
        vv_src = vscr.rearrange("s (c p) t -> s p c t", p=128)[sq_]
        for c4 in range(4):
            sch.op("sync", dma_fn(k_fm[:, 4 * c4:4 * (c4 + 1), :], kv_src[:, 4 * c4:4 * (c4 + 1), :]),
                   writes=["kfm"], dma=True, tag="kld")
            sch.op("sync", dma_fn(v_sb[:, 4 * c4:4 * (c4 + 1), :], vv_src[:, 4 * c4:4 * (c4 + 1), :]),
                   writes=["vsb"], dma=True, tag="vld")

        q_src = qscr.rearrange("s c p t -> s p c t")[sq_]
        for qt in range(16):
            p = qt % 2
            nk = qt + 1
            nkb = (nk * 128 + 511) // 512
            sch.op("sync", dma_fn(q_t[p][:, :, :], q_src[:, :, qt * 128:(qt + 1) * 128]),
                   reads=[f"qt{p}_r"], writes=[f"qt{p}"], dma=True, tag="qld")

            for kb in range(nkb):
                ncol = min(512, nk * 128 - kb * 512)

                def smm(p=p, kb=kb, ncol=ncol):
                    def f(t, g):
                        ins = None
                        for c in range(16):
                            ins = t.matmul(pall[:, kb * 512:kb * 512 + ncol],
                                           q_t[p][:, c, :],
                                           k_fm[:, c, kb * 512:kb * 512 + ncol],
                                           start=(c == 0), stop=(c == 15))
                        return ins
                    return f
                sch.op("tensor", smm(), reads=[f"qt{p}", "kfm"],
                       writes=[PTOK(kb)], tag="smm")
                if kb == nkb - 1:
                    def maskf(qt=qt, kb=kb):
                        def f(v, g):
                            return v.tensor_tensor(pall[:, qt * 128:(qt + 1) * 128],
                                                   pall[:, qt * 128:(qt + 1) * 128],
                                                   maskc[:], ALU.add)
                        return f
                    sch.op("vector", maskf(), reads=[PTOK(kb), "maskc"],
                           writes=[PTOK(kb)], small=True, tag="mask")

                def bmax(kb=kb, ncol=ncol):
                    def f(v, g):
                        return v.reduce_max(mxp[:, kb:kb + 1],
                                            pall[:, kb * 512:kb * 512 + ncol], AX)
                    return f
                sch.op("vector", bmax(), reads=[PTOK(kb)], writes=["mxp"],
                       small=True, tag="bmax")

            def nbf(p=p, nkb=nkb):
                def f(v, g):
                    ins = v.reduce_max(mxs[p][:], mxp[:, 0:nkb], AX)
                    g(ins)
                    return v.tensor_scalar(nbias[p][:], mxs[p][:], -QK_SCALE, None, ALU.mult)
                return f
            sch.op("vector", nbf(), reads=["mxp"], writes=[f"nb{p}", f"mx{p}"],
                   small=True, tag="nbias")

            def expf(p=p, nk=nk, qt=qt, sq_=sq_):
                def f(s, g):
                    return s.activation(probs[p][:, 0:nk * 128], pall[:, 0:nk * 128],
                                        AF.Exp, bias=nbias[p][:], scale=QK_SCALE,
                                        accum_out=rsum[:, sq_ * 16 + qt:sq_ * 16 + qt + 1])
                return f
            sch.op("scalar", expf(), reads=[PTOK(k) for k in range(nkb)] + [f"nb{p}"],
                   writes=[f"pr{p}", "rsum"], tag="exp")

            for half in range(0, nk, 8):
                hn = min(8, nk - half)
                bank = 6 + (half // 8)
                sch.op("tensor", transpose8_fn(PBb(bank), probs[p][:, half * 128:], hn, idb[:]),
                       reads=[f"pr{p}", "idb"], writes=[PTOK(bank)], tag="ptr")

                def pte(p=p, half=half, hn=hn, bank=bank):
                    dsl = probsT[p][:, half:half + hn, :]
                    def f(s, g):
                        return s.copy(dsl, PBb(bank)[:, 0:hn * 128]
                                      .rearrange("p (c t) -> p c t", t=128))
                    return f
                sch.op("scalar", pte(), reads=[PTOK(bank)], writes=[f"pT{p}"], tag="pte")

            for fb in range(4):
                bank = 4 + fb % 2

                def avmm(p=p, nk=nk, fb=fb, bank=bank):
                    def f(t, g):
                        ins = None
                        for kt in range(nk):
                            ins = t.matmul(PB(bank), probsT[p][:, kt, :],
                                           v_sb[:, kt, fb * 512:(fb + 1) * 512],
                                           start=(kt == 0), stop=(kt == nk - 1))
                        return ins
                    return f
                sch.op("tensor", avmm(), reads=[f"pT{p}", "vsb"], writes=[PTOK(bank)], tag="avmm")

                def avev(p=p, fb=fb, bank=bank):
                    def f(s, g):
                        return s.copy(av_tm[p][:, fb * 512:(fb + 1) * 512], PB(bank))
                    return f
                sch.op("scalar", avev(), reads=[PTOK(bank)], writes=[f"avt{p}"], tag="avev")

            for half in range(2):
                bank = 6 + half
                sch.op("tensor", transpose8_fn(PBb(bank), av_tm[p][:, half * 1024:], 8, idb[:]),
                       reads=[f"avt{p}", "idb"], writes=[PTOK(bank)], tag="avtr")

                def avte(p=p, half=half, bank=bank):
                    dsl = avst[p][:, half * 8:half * 8 + 8, :]
                    def f(s, g):
                        return s.copy(dsl, PBb(bank)[:, 0:1024]
                                      .rearrange("p (c t) -> p c t", t=128))
                    return f
                sch.op("scalar", avte(), reads=[PTOK(bank)], writes=[f"avs{p}"], tag="avte")
            av_dst = avscr.rearrange("s c p t -> s p c t")[sq_]
            sch.op("gpsimd", dma_fn(av_dst[:, :, qt * 128:(qt + 1) * 128], avst[p][:, :, :]),
                   reads=[f"avs{p}"], writes=[f"avw_{sq_}_{qt}"], dma=True, tag="avw")
            sch.op("vector", lambda e, g: e.nop(), reads=[f"qt{p}"], writes=[f"qt{p}_r"],
                   small=True, tag="qtr")
        A.pop()
        if rsumscr is not None:
            sch.op("gpsimd", dma_fn(rsumscr[sq_], rsum[:, :]), reads=["rsum"],
                   writes=[f"rsd{sq_}"], dma=True, tag="rsd")
        sch.barrier()

    A4_ON = "A4" in phases
    A.push()
    wo_sb = A.bf16(16 * 2048).rearrange("p (c t) -> p c t", c=16)
    wof = [A.f32(2048) for _ in range(2)]
    av_q = [A.bf16(16 * 128).rearrange("p (c t) -> p c t", c=16) for _ in range(2)]
    x1t = [A.f32(2048) for _ in range(2)]
    u4 = A.f32(2048)
    x1u = A.f32(2048)
    x2t = [A.f32(2048) for _ in range(2)]
    o2sb = [A.f32(512) for _ in range(2)]

    for c in (range(16) if A4_ON else ()):
        p = c % 2
        sch.op("sync", dma_fn(wof[p][:], wo_r[:, c, :]), reads=[f"wof{p}_r"],
               writes=[f"wof{p}"], dma=True, tag="wol")

        def woc(c=c, p=p):
            def f(v, g):
                return v.tensor_copy(wo_sb[:, c, :], wof[p][:])
            return f
        sch.op("vector", woc(), reads=[f"wof{p}"], writes=["wosb"], tag="woc")
        sch.op("vector", lambda e, g: e.nop(), reads=[f"wof{p}"], writes=[f"wof{p}_r"],
               small=True, tag="wor")

    av_src4 = avscr.rearrange("s c p t -> s p c t")
    for idx in (range(32) if A4_ON else ()):
        sq_, qt = idx // 16, idx % 16
        p = idx % 2
        sch.op("sync", dma_fn(av_q[p][:, :, :],
                              av_src4[sq_][:, :, qt * 128:(qt + 1) * 128]),
               reads=[f"avq{p}_r"], writes=[f"avq{p}"], dma=True, tag="avl")
        sch.op("gpsimd", dma_fn(x1t[p][:], x1s[idx]), reads=[f"x1t{p}_r"],
               writes=[f"x1t{p}"], dma=True, tag="x1l")
        stt = stats[p]
        sch.op("vector", ln_stats_fn(x1t[p], stt, EPS_LN),
               reads=[f"x1t{p}"], writes=[stt.tok], small=True, tag="a4st")
        sch.op("scalar", apply_fn(u4[:], x1t[p], stt),
               reads=[f"x1t{p}", stt.tok], writes=["u4"], tag="a4ap")

        def addf(p=p):
            def f(e, g):
                return e.tensor_add(x1u[:], x1t[p][:], u4[:])
            return f
        sch.op("gpsimd", addf(), reads=[f"x1t{p}", "u4"], writes=["x1u"], tag="a4add")
        sch.op("vector", lambda e, g: e.nop(), reads=[f"x1t{p}"], writes=[f"x1t{p}_r"],
               small=True, tag="x1r")

        def rcf(p=p, idx=idx):
            def f(v, g):
                return v.reciprocal(rcps[p][:], rsum[:, idx:idx + 1])
            return f
        sch.op("vector", rcf(), reads=["rsum"], writes=[f"rcp{p}"], small=True, tag="rcp")

        for fb in range(4):
            bank = fb

            def omm(p=p, fb=fb, bank=bank):
                def f(t, g):
                    ins = None
                    for c in range(16):
                        ins = t.matmul(PB(bank), av_q[p][:, c, :],
                                       wo_sb[:, c, fb * 512:(fb + 1) * 512],
                                       start=(c == 0), stop=(c == 15))
                    return ins
                return f
            sch.op("tensor", omm(), reads=[f"avq{p}", "wosb"], writes=[PTOK(bank)], tag="omm")

            def oev(p=p, fb=fb, bank=bank):
                def f(s, g):
                    return s.activation(o2sb[fb % 2][:], PB(bank), AF.Identity,
                                        scale=rcps[p][:])
                return f
            sch.op("scalar", oev(), reads=[PTOK(bank), f"rcp{p}"],
                   writes=[f"o2sb{fb % 2}"], tag="oev")

            def epi(p=p, fb=fb):
                def f(v, g):
                    return v.tensor_add(x2t[p][:, fb * 512:(fb + 1) * 512],
                                        o2sb[fb % 2][:],
                                        x1u[:, fb * 512:(fb + 1) * 512])
                return f
            sch.op("vector", epi(), reads=[f"o2sb{fb % 2}", "x1u"],
                   writes=[f"x2t{p}"], tag="epi")
        sch.op("vector", lambda e, g: e.nop(), reads=[f"avq{p}"], writes=[f"avq{p}_r"],
               small=True, tag="avqr")
        sch.op("gpsimd", dma_fn(x2s[idx], x2t[p][:]), reads=[f"x2t{p}"],
               writes=[f"x2w{idx}"], dma=True, tag="x2w")
    A.pop()
    if A4_ON:
        sch.barrier()

    if "F2" in phases:
        emit_ffn(x2s, out.rearrange("b (n p) h -> (b n) p h", p=128), "F2")
    else:
        A.push()
        zt = A.f32(2048)
        sch.op("vector", lambda e, g: e.memset(zt, 0.0), writes=["zt"], tag="z")
        o3 = out.rearrange("b (n p) h -> (b n) p h", p=128)
        for tile in range(NT):
            sch.op("gpsimd", dma_fn(o3[tile], zt), reads=["zt"],
                   writes=[f"zo{tile}"], dma=True, tag="zo")
        A.pop()

    sch.finalize()

    sems = {}
    for eng in Sched.ENG:
        sems[(eng, False)] = en(nc.semaphore(f"se_{eng}"))
        sems[(eng, True)] = en(nc.semaphore(f"sd_{eng}"))
    hz = {eng: en(nc.semaphore(f"hz_{eng}")) for eng in ("vector", "scalar", "gpsimd")}
    hz_cnt = {eng: [0] for eng in hz}

    block = en(nc.Block())
    per_eng = {eng: [o for o in sch.ops if o["eng"] == eng] for eng in Sched.ENG}

    def make_body(eng):
        ops = per_eng[eng]
        def body(e):
            hw = {}
            if eng in hz:
                cnt = hz_cnt[eng]
                sem = hz[eng]
                def guard(ins):
                    cnt[0] += 1
                    ins.then_inc(sem, 1)
                    e.wait_ge(sem, cnt[0])
            else:
                def guard(ins):
                    pass
            for o in ops:
                for (peng, pdma, val) in sch.needed_waits(o):
                    key = (peng, pdma)
                    if hw.get(key, 0) < val:
                        e.wait_ge(sems[key], val)
                        hw[key] = val
                ins = o["fn"](e, guard)
                if o["inc"]:
                    ins.then_inc(sems[(eng, o["dma"])], 16 if o["dma"] else 1)
            for key, total in sch.final_counts.items():
                if hw.get(key, 0) < total:
                    e.wait_ge(sems[key], total)
        return body

    block.tensor(make_body("tensor"))
    block.vector(make_body("vector"))
    block.scalar(make_body("scalar"))
    block.sync(make_body("sync"))
    block.gpsimd(make_body("gpsimd"))

    st.close()
    return nc


_NC_CACHE = {}


def _get_program(dbg=(), phases=("P0", "F1", "A", "A4", "F2")):
    key = (tuple(sorted(dbg)), tuple(phases))
    if key not in _NC_CACHE:
        _NC_CACHE[key] = build_core_program(dbg, phases)
    return _NC_CACHE[key]


def kernel(L0_attn_x, L0_attn_qkv, L0_attn_o_proj, L0_attn_gamma, L0_attn_beta,
           L0_ffn_gate_up_packed, L0_ffn_gate_up_scales, L0_ffn_down_packed,
           L0_ffn_down_scales, _dbg=(), _run=True,
           _phases=("P0", "F1", "A", "A4", "F2")):
    x = np.ascontiguousarray(np.asarray(L0_attn_x, dtype=np.float32))
    qkv = np.ascontiguousarray(np.asarray(L0_attn_qkv, dtype=np.float32))
    wo = np.ascontiguousarray(np.asarray(L0_attn_o_proj, dtype=np.float32))
    gamma = np.ascontiguousarray(np.asarray(L0_attn_gamma, dtype=np.float32))
    beta = np.ascontiguousarray(np.asarray(L0_attn_beta, dtype=np.float32))
    gup = np.ascontiguousarray(np.asarray(L0_ffn_gate_up_packed, dtype=np.int32))
    gus = np.ascontiguousarray(np.asarray(L0_ffn_gate_up_scales, dtype=np.float16))
    dnp = np.ascontiguousarray(np.asarray(L0_ffn_down_packed, dtype=np.int32))
    dns = np.ascontiguousarray(np.asarray(L0_ffn_down_scales, dtype=np.float16))

    nc = _get_program(_dbg, _phases)
    in_maps = []
    for c in range(NCORES):
        in_maps.append({
            "x": x[SEQ * c:SEQ * (c + 1)],
            "qkv": qkv, "wo": wo, "gamma": gamma, "beta": beta,
            "gup": np.ascontiguousarray(gup[:, 32 * c:32 * (c + 1)]),
            "gus": np.ascontiguousarray(gus[2 * c:2 * (c + 1), :]),
            "dnp": np.ascontiguousarray(dnp[256 * c:256 * (c + 1), :]),
            "dns": np.ascontiguousarray(dns[:, 256 * c:256 * (c + 1)]),
        })
    if not _run:
        return nc, in_maps
    res = run_bass_kernel_spmd(nc, in_maps, core_ids=list(range(NCORES)))
    outs = np.concatenate([r["out"] for r in res.results], axis=0)
    if _dbg:
        return outs.reshape(16, 2048, 2048), res
    return outs.reshape(16, 2048, 2048)


# version 22
# speedup vs baseline: 1.0234x; 1.0234x over previous
import contextlib
import numpy as np

import concourse.bass as bass
import concourse.mybir as mybir
from concourse.bass_utils import run_bass_kernel_spmd

F32 = mybir.dt.float32
BF16 = mybir.dt.bfloat16
U16 = mybir.dt.uint16
I32 = mybir.dt.int32
F16 = mybir.dt.float16
AF = mybir.ActivationFunctionType
ALU = mybir.AluOpType
AX = mybir.AxisListType.X

NCORES = 8
SEQ = 2
S = 2048
H = 2048
I = 5632
I2 = 2 * I
T = SEQ * S
NT = T // 128
GTILES = 4
NGRP = NT // GTILES
GUPW = H // 8
DNPW = I // 8
NEWTON = 2
QK_SCALE = 1.0 / float(np.sqrt(2048.0))
EPS_LN = 1e-5
EPS_RMS = 1e-6
MAGIC = 0x5F3759DF



SERIAL = False
FULL_EDGES = False
FULL_EDGE_ENGINES = ("vector", "gpsimd")


class Sched:
    ENG = ("tensor", "vector", "scalar", "sync", "gpsimd")

    def __init__(self):
        self.ops = []
        self.last_w = {}
        self.readers = {}
        self.barrier_op = None

    def op(self, eng, fn, reads=(), writes=(), dma=False, small=False, tag=""):
        i = len(self.ops)
        deps = set()
        for t in reads:
            j = self.last_w.get(t)
            if j is not None:
                deps.add(j)
        for t in writes:
            j = self.last_w.get(t)
            if j is not None:
                deps.add(j)
            for j2 in self.readers.get(t, ()):
                deps.add(j2)
        if self.barrier_op is not None:
            deps.add(self.barrier_op)
        if SERIAL and i > 0:
            deps.add(i - 1)
        deps.discard(i)
        o = dict(i=i, eng=eng, fn=fn, deps=sorted(deps), dma=dma, small=small,
                 inc=bool(dma), val=None, tag=tag)
        self.ops.append(o)
        for t in reads:
            self.readers.setdefault(t, []).append(i)
        for t in writes:
            self.last_w[t] = i
            self.readers[t] = []
        return i

    def barrier(self):
        last = {}
        for o in self.ops:
            last[o["eng"]] = o["i"]
        deps = sorted(last.values())
        i = len(self.ops)
        o = dict(i=i, eng="vector", fn=lambda e, g: e.nop(), deps=deps,
                 dma=False, small=False, inc=False, val=None, tag="barrier")
        self.ops.append(o)
        self.barrier_op = i

    def finalize(self):
        for o in self.ops:
            for j in o["deps"]:
                p = self.ops[j]
                if p["inc"]:
                    continue
                if p["dma"] or p["eng"] != o["eng"]:
                    p["inc"] = True
                elif p["eng"] in ("vector", "gpsimd") and p["small"]:
                    p["inc"] = True
        cnt = {}
        for o in self.ops:
            if o["inc"]:
                key = (o["eng"], o["dma"])
                step = 16 if o["dma"] else 1
                cnt[key] = cnt.get(key, 0) + step
                o["val"] = cnt[key]
        self.final_counts = cnt

    def needed_waits(self, o):
        ws = []
        for j in o["deps"]:
            p = self.ops[j]
            if not p["inc"]:
                continue
            same = p["eng"] == o["eng"]
            if same and not p["dma"] and not SERIAL and not FULL_EDGES \
                    and p["eng"] not in FULL_EDGE_ENGINES:
                if not (p["eng"] in ("vector", "gpsimd") and p["small"]):
                    continue
            ws.append((p["eng"], p["dma"], p["val"]))
        return ws


def build_core_program(dbg=(), phases=("P0", "F1", "A", "A4", "F2")):
    nc = bass.Bass("TRN2", target_bir_lowering=False)
    kind = lambda name: "ExternalOutput" if name in dbg else "Internal"

    x_in = nc.dram_tensor("x", [SEQ, S, H], F32, kind="ExternalInput")
    qkv = nc.dram_tensor("qkv", [H, 3 * H], F32, kind="ExternalInput")
    wo = nc.dram_tensor("wo", [H, H], F32, kind="ExternalInput")
    gamma = nc.dram_tensor("gamma", [H], F32, kind="ExternalInput")
    beta = nc.dram_tensor("beta", [H], F32, kind="ExternalInput")
    gup = nc.dram_tensor("gup", [I2, GUPW // 8], I32, kind="ExternalInput")
    gus = nc.dram_tensor("gus", [2, I2], F16, kind="ExternalInput")
    dnp = nc.dram_tensor("dnp", [H // 8, DNPW], I32, kind="ExternalInput")
    dns = nc.dram_tensor("dns", [44, H // 8], F16, kind="ExternalInput")
    out = nc.dram_tensor("out", [SEQ, S, H], F32, kind="ExternalOutput")

    wguT = nc.dram_tensor("wguT", [H, I2], BF16, kind="Internal")
    wdnT8 = nc.dram_tensor("wdnT8", [8, I, H // 8], BF16, kind="Internal")
    locg = nc.dram_tensor("locg", [H // 8, I2], BF16, kind="Internal")
    locd = nc.dram_tensor("locd", [I, H // 8], BF16, kind="Internal")
    x1s = nc.dram_tensor("x1s", [NT, 128, H], F32, kind=kind("x1s"))
    x2s = nc.dram_tensor("x2s", [NT, 128, H], F32, kind=kind("x2s"))
    qscr = nc.dram_tensor("qscr", [SEQ, 16, 128, S], BF16, kind=kind("qscr"))
    kscr = nc.dram_tensor("kscr", [SEQ, 16, 128, S], BF16, kind=kind("kscr"))
    vscr = nc.dram_tensor("vscr", [SEQ, S, H], BF16, kind=kind("vscr"))
    avscr = nc.dram_tensor("avscr", [SEQ, 16, 128, S], BF16, kind=kind("avscr"))
    xnscr = None
    if "xnscr" in dbg:
        xnscr = nc.dram_tensor("xnscr", [SEQ, 16, 128, S], BF16, kind="ExternalOutput")
    if "wgdbg" in dbg:
        wgdbg = nc.dram_tensor("wgdbg", [H, I2], BF16, kind="ExternalOutput")
        wddbg = nc.dram_tensor("wddbg", [8, I, H // 8], BF16, kind="ExternalOutput")
    rsumscr = None
    if "rsumscr" in dbg:
        rsumscr = nc.dram_tensor("rsumscr", [2, 128, 32], F32, kind="ExternalOutput")

    sch = Sched()
    st = contextlib.ExitStack()
    en = st.enter_context

    iot = en(nc.sbuf_tensor("iot", [128, 128], I32))
    idb = en(nc.sbuf_tensor("idb", [128, 128], BF16))
    idf = en(nc.sbuf_tensor("idf", [128, 128], F32))
    maskc = en(nc.sbuf_tensor("maskc", [128, 128], F32))
    rsum = en(nc.sbuf_tensor("rsum", [128, 32], F32))
    mxp = en(nc.sbuf_tensor("mxp", [128, 4], F32))
    sc16g = en(nc.sbuf_tensor("sc16g", [2, 512], F16))
    sc32g = en(nc.sbuf_tensor("sc32g", [2, 512], F32))
    sc16d = en(nc.sbuf_tensor("sc16d", [44, 128], F16))
    sc32d = en(nc.sbuf_tensor("sc32d", [44, 128], F32))
    scTg = [en(nc.sbuf_tensor(f"scTg{p}", [128, 16], F32)) for p in range(2)]
    scTd = [en(nc.sbuf_tensor(f"scTd{p}", [128, 44], F32)) for p in range(2)]

    class StatSet:
        def __init__(self, p):
            self.bns = en(nc.sbuf_tensor(f"bns{p}", [128, 24], F32))
            self.mv = en(nc.sbuf_tensor(f"mv{p}", [128, 2], F32))
            self.vp = en(nc.sbuf_tensor(f"vp{p}", [128, 1], F32))
            self.t1 = en(nc.sbuf_tensor(f"t1_{p}", [128, 1], I32))
            self.r = en(nc.sbuf_tensor(f"r{p}", [128, 1], F32))
            self.hc = en(nc.sbuf_tensor(f"hc{p}", [128, 1], F32))
            self.nmr = en(nc.sbuf_tensor(f"nmr{p}", [128, 1], F32))
            self.tok = f"stats{p}"
    stats = [StatSet(p) for p in range(4)]
    mxs = [en(nc.sbuf_tensor(f"mx{p}", [128, 1], F32)) for p in range(2)]
    nbias = [en(nc.sbuf_tensor(f"nb{p}", [128, 1], F32)) for p in range(2)]
    rcps = [en(nc.sbuf_tensor(f"rcp{p}", [128, 1], F32)) for p in range(2)]

    pall = en(nc.psum_tensor("pall", [128, 4096], F32))
    PB = lambda k: pall[:, 512 * k:512 * (k + 1)]
    PBb = lambda k: pall[:, 512 * k:512 * (k + 1)].bitcast(BF16)
    PTOK = lambda k: f"P{k}"

    ACOLS = 51264
    arena_t = en(nc.sbuf_tensor("arena", [128, ACOLS], F32))

    class Arena:
        def __init__(self):
            self.base = 0
            self.stack = []
        def push(self):
            self.stack.append(self.base)
        def pop(self):
            self.base = self.stack.pop()
        def f32(self, n):
            o = self.base
            self.base += n
            assert self.base <= ACOLS, f"arena overflow {self.base}"
            return arena_t[:, o:o + n]
        def bf16(self, n):
            assert n % 2 == 0
            return self.f32(n // 2).bitcast(BF16)
        def u16(self, n):
            assert n % 2 == 0
            return self.f32(n // 2).bitcast(U16)
        def i32(self, n):
            return self.f32(n).bitcast(I32)
    A = Arena()

    def f_iota(e, g):
        return e.iota(iot[:], [[-1, 128]], base=0, channel_multiplier=1)
    sch.op("gpsimd", f_iota, writes=["iot"], tag="iota")

    def f_ids(e, g):
        e.tensor_scalar(idf[:], iot[:], 0, 1.0, ALU.is_equal, ALU.mult)
        e.tensor_scalar(idb[:], iot[:], 0, 1.0, ALU.is_equal, ALU.mult)
        return e.tensor_scalar(maskc[:], iot[:], 0, -1e30, ALU.is_lt, ALU.mult)
    sch.op("vector", f_ids, reads=["iot"], writes=["idf", "idb", "maskc"], tag="ids")

    def newton_rsqrt(v, g, stt, src_ap, scale, bias):
        ins = v.tensor_scalar(stt.vp[:], src_ap, scale, bias, ALU.mult, ALU.add)
        g(ins)
        ins = v.tensor_scalar(stt.t1[:], stt.vp[:].bitcast(I32), 1, None,
                              ALU.logical_shift_right)
        g(ins)
        ins = v.tensor_scalar(stt.t1[:], stt.t1[:], -1, MAGIC, ALU.mult, ALU.add)
        g(ins)
        for it in range(NEWTON):
            rin = stt.t1[:].bitcast(F32) if it == 0 else stt.r[:]
            ins = v.tensor_tensor(stt.hc[:], rin, rin, ALU.mult); g(ins)
            ins = v.tensor_tensor(stt.hc[:], stt.hc[:], stt.vp[:], ALU.mult); g(ins)
            ins = v.tensor_scalar(stt.hc[:], stt.hc[:], -0.5, 1.5, ALU.mult, ALU.add); g(ins)
            ins = v.tensor_tensor(stt.r[:], rin, stt.hc[:], ALU.mult)
            if it < NEWTON - 1:
                g(ins)
        return ins

    def ln_stats_fn(xap, stt, eps):
        def f(v, g):
            for c in range(4):
                ins = v.bn_stats(stt.bns[:, 6 * c:6 * (c + 1)], xap[:, 512 * c:512 * (c + 1)])
            g(ins)
            ins = v.bn_aggr(stt.mv[:], stt.bns[:]); g(ins)
            ins = newton_rsqrt(v, g, stt, stt.mv[:, 1:2], 1.0, eps)
            g(ins)
            return v.tensor_scalar(stt.nmr[:], stt.mv[:, 0:1], stt.r[:], -1.0,
                                   ALU.mult, ALU.mult)
        return f

    def rms_stats_fn(ssq_ap, stt, n, eps):
        def f(v, g):
            return newton_rsqrt(v, g, stt, ssq_ap, 1.0 / n, eps)
        return f

    def apply_fn(dst, src, stt):
        def f(v, g):
            return v.tensor_scalar(dst, src, stt.r[:], stt.nmr[:], ALU.mult, ALU.add)
        return f

    def dma_fn(e_dst, e_src):
        def f(e, g):
            return e.dma_start(e_dst, e_src)
        return f

    def transpose8_fn(pbank_bf, src, n, ident, width=128):
        def f(t, g):
            ins = None
            for j in range(n):
                ins = t.transpose(pbank_bf[:, j * width:(j + 1) * width],
                                  src[:, j * width:(j + 1) * width], ident)
            return ins
        return f

    def copy_fn(dst, src):
        def f(e, g):
            return e.tensor_copy(dst, src)
        return f

    def scopy_fn(dst, src):
        def f(e, g):
            return e.copy(dst, src)
        return f

    A.push()
    pk4 = [A.i32(8 * 32) for _ in range(2)]
    pkd = [A.i32(DNPW) for _ in range(2)]
    tsetV = (A.u16(2 * DNPW), A.u16(2 * DNPW), A.bf16(2 * DNPW),
             A.bf16(2 * DNPW), A.u16(2 * DNPW))
    valsg = A.bf16(8 * 256)
    svalg = [A.bf16(8 * 256) for _ in range(2)]
    valsd = A.bf16(I)
    svald = [A.bf16(I) for _ in range(2)]
    stagg = [A.bf16(2 * 1024) for _ in range(2)]
    stagd = A.bf16(44 * 256)

    def decode_fn(pk, nwords, vals_ap, ks, tset):
        nh = 2 * nwords
        pku = pk.bitcast(U16)
        va = vals_ap.bitcast(U16).rearrange("p (j k) -> p j k", k=4)
        ta, tg, tt2, tfm, tsb = tset

        def f(v, g):
            ins = None
            for k in ks:
                a = ta[:, :nh]
                if 4 * k <= 6:
                    v.tensor_scalar(a, pku, 6 - 4 * k, 0x01C0,
                                    ALU.logical_shift_left, ALU.bitwise_and)
                else:
                    v.tensor_scalar(a, pku, 4 * k - 6, 0x01C0,
                                    ALU.logical_shift_right, ALU.bitwise_and)
                gg = tg[:, :nh]
                v.tensor_scalar(gg, a, 0x3F00, None, ALU.add)
                gbf = gg.bitcast(BF16)
                v.tensor_scalar(tt2[:, :nh], gbf, 2.0, -1.0, ALU.mult, ALU.add)
                v.tensor_tensor(tfm[:, :nh], gbf, tt2[:, :nh], ALU.min)
                v.tensor_scalar(tsb[:, :nh], pku, 1 << (4 * k + 3), 12 - 4 * k,
                                ALU.bitwise_and, ALU.logical_shift_left)
                ins = v.tensor_tensor(va[:, :, k], tfm[:, :nh].bitcast(U16),
                                      tsb[:, :nh], ALU.bitwise_or)
            return ins
        return f

    def scale_fn(pairs, scT_ap):
        def f(v, g):
            ins = None
            for dst, s_ap, gi in pairs:
                ins = v.tensor_scalar(dst, s_ap, scT_ap[:, gi:gi + 1], None, ALU.mult)
            return ins
        return f

    def p0_emit():
        gup_r = gup.rearrange("(t p) w -> p t w", p=128)
        for g4 in range(11):
            pk = pk4[g4 % 2]
            pktok = f"pk{g4 % 2}"
            pk3 = pk.rearrange("p (t w) -> p t w", t=8)
            sch.op("sync", dma_fn(pk3, gup_r[:, 8 * g4:8 * (g4 + 1), :]),
                   writes=[pktok], dma=True, tag="pkl")
            for hh in range(2):
                sch.op("sync", dma_fn(sc16g[:, :],
                                      gus[:, 1024 * g4 + 512 * hh:1024 * g4 + 512 * (hh + 1)]),
                       reads=["sc32g"], writes=["sc16g"], dma=True, tag="scl")
                sch.op("vector", copy_fn(sc32g[:, :], sc16g[:, :]),
                       reads=["sc16g"], writes=["sc32g"], tag="scc")

                def sctr(hh=hh):
                    def f(t, g):
                        ins = None
                        for j in range(4):
                            ins = t.transpose(PB(6)[:, 8 * hh + 2 * j:8 * hh + 2 * j + 2],
                                              sc32g[:, 128 * j:128 * (j + 1)], idf[:2, :2])
                        return ins
                    return f
                sch.op("tensor", sctr(), reads=["sc32g", "idf"], writes=[PTOK(6)], tag="sctr")
                sch.op("scalar", scopy_fn(scTg[g4 % 2][:, 8 * hh:8 * (hh + 1)],
                                          PB(6)[:, 8 * hh:8 * (hh + 1)]),
                       reads=[PTOK(6)], writes=[f"scT{g4 % 2}"], tag="sce")
            sch.op("vector", decode_fn(pk, 8 * 32, valsg[:, :], (0, 1, 2, 3), tsetV),
                   reads=[pktok], writes=["valsV"], tag="dec")
            sva = svalg[g4 % 2]
            v3 = valsg.rearrange("p (t h) -> p t h", t=8)
            s3 = sva.rearrange("p (t h) -> p t h", t=8)
            pairs = [(s3[:, j, 128 * gi:128 * (gi + 1)], v3[:, j, 128 * gi:128 * (gi + 1)],
                      2 * j + gi) for j in range(8) for gi in range(2)]
            sch.op("vector", scale_fn(pairs, scTg[g4 % 2][:]),
                   reads=["valsV", f"scT{g4 % 2}"], writes=[f"sval{g4 % 2}"], tag="scale")
            stg = stagg[g4 % 2]
            stg3 = stg.rearrange("p (c o) -> p c o", c=2)
            for half in range(2):
                bank = 6 + half

                def wtr(half=half, bank=bank, s3=s3):
                    def f(t, g):
                        ins = None
                        for j in range(8):
                            ins = t.transpose(
                                PBb(bank)[:, 128 * j:128 * (j + 1)],
                                s3[:, j, 128 * half:128 * (half + 1)], idb[:])
                        return ins
                    return f
                sch.op("tensor", wtr(), reads=[f"sval{g4 % 2}", "idb"],
                       writes=[PTOK(bank)], tag="wtr")

                def wte(half=half, bank=bank, stg3=stg3):
                    def f(s, g):
                        return s.copy(stg3[:, half, :], PBb(bank)[:, 0:1024])
                    return f
                sch.op("scalar", wte(), reads=[PTOK(bank)],
                       writes=[f"stagg{g4 % 2}"], tag="wte")
            for c in range(2):
                dsl = locg[c * 128:(c + 1) * 128, 1024 * g4:1024 * (g4 + 1)]
                sch.op("gpsimd", dma_fn(dsl, stg3[:, c, :]), reads=[f"stagg{g4 % 2}"],
                       writes=[f"locg_{g4}_{c}"], dma=True, tag="wst")
        def ccg(e, g):
            return e.collective_compute(
                "AllGather", mybir.AluOpType.bypass,
                replica_groups=[list(range(NCORES))],
                ins=[locg.ap().opt()], outs=[wguT.ap().opt()])
        sch.op("gpsimd", ccg,
               reads=[f"locg_{g4}_{c}" for g4 in range(11) for c in range(2)],
               writes=["wguT_cc"], tag="ccg")

        dnp_r = dnp.rearrange("(t p) w -> p t w", p=128)
        stagd3 = stagd.rearrange("p (c o) -> p c o", c=44)
        for dt_ in range(2):
            pk = pkd[dt_ % 2]
            pktok = f"pkd{dt_ % 2}"
            sch.op("sync", dma_fn(pk[:, :], dnp_r[:, dt_, :]),
                   writes=[pktok], dma=True, tag="pkdl")
            sch.op("sync", dma_fn(sc16d[:, :], dns[:, 128 * dt_:128 * (dt_ + 1)]),
                   reads=["sc32d"], writes=["sc16d"], dma=True, tag="sdl")
            sch.op("vector", copy_fn(sc32d[:, :], sc16d[:, :]),
                   reads=["sc16d"], writes=["sc32d"], tag="sdc")

            def sdtr(dt_=dt_):
                def f(t, g):
                    return t.transpose(PB(6)[:, 0:44], sc32d[:, :], idf[:44, :44])
                return f
            sch.op("tensor", sdtr(), reads=["sc32d", "idf"], writes=[PTOK(6)], tag="sdtr")
            sch.op("scalar", scopy_fn(scTd[dt_ % 2][:, :], PB(6)[:, 0:44]),
                   reads=[PTOK(6)], writes=[f"scTd{dt_ % 2}"], tag="sde")
            sch.op("vector", decode_fn(pk[:, :], DNPW, valsd[:, :], (0, 1, 2, 3), tsetV),
                   reads=[pktok], writes=["valsV"], tag="ddec")
            sva = svald[dt_ % 2]
            pairs = [(sva[:, 128 * gi:128 * (gi + 1)], valsd[:, 128 * gi:128 * (gi + 1)], gi)
                     for gi in range(44)]
            sch.op("vector", scale_fn(pairs, scTd[dt_ % 2][:]),
                   reads=["valsV", f"scTd{dt_ % 2}"], writes=[f"svald{dt_ % 2}"], tag="dscale")
            for half in range(0, 44, 8):
                hn = min(8, 44 - half)
                bank = 6 + (half // 8) % 2
                sch.op("tensor", transpose8_fn(PBb(bank), sva[:, half * 128:], hn, idb[:]),
                       reads=[f"svald{dt_ % 2}", "idb"], writes=[PTOK(bank)], tag="dwtr")

                def dwte(half=half, hn=hn, bank=bank, dt_=dt_):
                    dsl = stagd3[:, half:half + hn, 128 * dt_:128 * (dt_ + 1)]
                    def f(s, g):
                        return s.copy(dsl, PBb(bank)[:, 0:hn * 128]
                                      .rearrange("p (c o) -> p c o", o=128))
                    return f
                sch.op("scalar", dwte(), reads=[PTOK(bank)], writes=["stagd"], tag="dwte")
        for c in range(44):
            dsl = locd[c * 128:(c + 1) * 128, :]
            sch.op("gpsimd", dma_fn(dsl, stagd3[:, c, :]), reads=["stagd"],
                   writes=[f"locd_{c}"], dma=True, tag="dst")
        def ccd(e, g):
            return e.collective_compute(
                "AllGather", mybir.AluOpType.bypass,
                replica_groups=[list(range(NCORES))],
                ins=[locd.ap().opt()],
                outs=[wdnT8.rearrange("r i o -> (r i) o").opt()])
        sch.op("gpsimd", ccd, reads=[f"locd_{c}" for c in range(44)],
               writes=["wdnT_cc"], tag="ccd")

    if "P0" in phases:
        p0_emit()
        if "wgdbg" in dbg:
            for c in range(16):
                sch.op("gpsimd", dma_fn(wgdbg[c * 128:(c + 1) * 128, :],
                                        wguT[c * 128:(c + 1) * 128, :]),
                       reads=["wguT_cc"], writes=[f"wgd{c}"], dma=True, tag="wgd")
            for rr in range(8):
                sch.op("gpsimd", dma_fn(wddbg[rr], wdnT8[rr]),
                       reads=["wdnT_cc"], writes=[f"wdd{rr}"], dma=True, tag="wdd")
        sch.barrier()
    A.pop()

    def emit_ffn(xsrc, xdst, phase):
        A.push()
        u_fm = A.bf16(16 * 512).rearrange("p (c t) -> p c t", c=16)
        h_act = A.bf16(44 * 512).rearrange("p (c t) -> p c t", c=44)
        wg = [A.bf16(16 * 256).rearrange("p (c t) -> p c t", c=16) for _ in range(2)]
        wu = [A.bf16(16 * 256).rearrange("p (c t) -> p c t", c=16) for _ in range(2)]
        wd = [A.bf16(44 * 128).rearrange("p (c t) -> p c t", c=44) for _ in range(2)]
        y_tm = A.bf16(GTILES * 2048).rearrange("p (j h) -> p j h", j=GTILES)
        yfm = [A.bf16(512) for _ in range(2)]
        xt2 = [A.f32(2048) for _ in range(2)]
        xt4 = [A.f32(2048) for _ in range(2)]
        outt = [A.f32(2048)]
        utm = [A.bf16(2048) for _ in range(4)]
        gsil = [A.bf16(512) for _ in range(2)]

        def stage1a(grp):
            for tt in range(GTILES):
                tile = grp * GTILES + tt
                xb = xt2[tt % 2]
                xtok = f"{phase}xt{tt % 2}"
                stt = stats[tt % 2]
                sch.op("gpsimd", dma_fn(xb, xsrc[tile]), writes=[xtok], dma=True, tag="xld")
                sch.op("vector", ln_stats_fn(xb, stt, EPS_LN),
                       reads=[xtok], writes=[stt.tok], small=True, tag="lnst")
                sch.op("vector", apply_fn(utm[tt][:], xb, stt),
                       reads=[xtok, stt.tok], writes=[f"{phase}utm{tt}"], tag="uap")

        def stage1b(grp):
            for tt in range(GTILES):
                for half in range(2):
                    bank = 0 + half
                    sch.op("tensor",
                           transpose8_fn(PBb(bank), utm[tt][:, half * 1024:], 8, idb[:]),
                           reads=[f"{phase}utm{tt}", "idb"], writes=[PTOK(bank)], tag="utr")
                    dsl = u_fm[:, half * 8:half * 8 + 8, tt * 128:(tt + 1) * 128]
                    sch.op("vector", copy_fn(dsl, PBb(bank)[:, 0:1024]),
                           reads=[PTOK(bank)], writes=["ufm"], tag="ute")

        def load2(b):
            wgb, wub = wg[b % 2], wu[b % 2]
            wgt, wut = f"wg{b % 2}", f"wu{b % 2}"
            src_g = wguT.rearrange("(c p) o -> p c o", p=128)[:, :, 256 * b:256 * (b + 1)]
            src_u = wguT.rearrange("(c p) o -> p c o", p=128)[:, :, I + 256 * b:I + 256 * (b + 1)]
            sch.op("sync", dma_fn(wgb, src_g), reads=["wguT_cc"], writes=[wgt], dma=True, tag="wgl")
            sch.op("sync", dma_fn(wub, src_u), reads=["wguT_cc"], writes=[wut], dma=True, tag="wul")

        def stage2(grp, preloaded):
            for b in range(22):
                wgb, wub = wg[b % 2], wu[b % 2]
                wgt, wut = f"wg{b % 2}", f"wu{b % 2}"
                if not (preloaded and b < 2):
                    load2(b)
                for s_ in range(2):
                    bg, bu = (0, 1) if s_ == 0 else (2, 3)
                    isl = 2 * b + s_

                    def mm32(bg=bg, bu=bu, wgb=wgb, wub=wub, s_=s_):
                        def f(t, g):
                            for c in range(16):
                                t.matmul(PB(bg), wgb[:, c, s_ * 128:(s_ + 1) * 128],
                                         u_fm[:, c, :], start=(c == 0), stop=(c == 15))
                            ins = None
                            for c in range(16):
                                ins = t.matmul(PB(bu), wub[:, c, s_ * 128:(s_ + 1) * 128],
                                               u_fm[:, c, :], start=(c == 0), stop=(c == 15))
                            return ins
                        return f
                    sch.op("tensor", mm32(), reads=[wgt, wut, "ufm"],
                           writes=[PTOK(bg), PTOK(bu)], tag="gumm")

                    def silu_fn(bg=bg, p=isl % 2):
                        def f(s, g):
                            return s.activation(gsil[p][:], PB(bg), AF.Silu)
                        return f
                    sch.op("scalar", silu_fn(), reads=[PTOK(bg)],
                           writes=[f"gs{isl % 2}"], tag="silu")

                    def hmul_fn(bu=bu, isl=isl, p=isl % 2):
                        def f(v, g):
                            return v.tensor_tensor(h_act[:, isl, :], PB(bu),
                                                   gsil[p][:], ALU.mult)
                        return f
                    sch.op("vector", hmul_fn(), reads=[PTOK(bu), f"gs{isl % 2}"],
                           writes=["hact"], tag="hmul")

        def stage3(grp):
            dsrc = wdnT8.rearrange("r (c p) o -> r p c o", p=128)
            for hs in range(16):
                wdb = wd[hs % 2]
                wdt = f"wd{hs % 2}"
                src_d = dsrc[hs // 2][:, :, (hs % 2) * 128:(hs % 2) * 128 + 128]
                sch.op("sync", dma_fn(wdb, src_d), reads=["wdnT_cc"], writes=[wdt], dma=True, tag="wdl")
                bank_y = 4 + (hs % 2)

                def dmm(wdb=wdb, bank_y=bank_y):
                    def f(t, g):
                        ins = None
                        for c in range(44):
                            ins = t.matmul(PB(bank_y), wdb[:, c, :],
                                           h_act[:, c, :], start=(c == 0), stop=(c == 43))
                        return ins
                    return f
                sch.op("tensor", dmm(), reads=[wdt, "hact"],
                       writes=[PTOK(bank_y)], tag="dmm")
                yf = yfm[hs % 2]
                sch.op("scalar", scopy_fn(yf[:], PB(bank_y)),
                       reads=[PTOK(bank_y)], writes=[f"yfm{hs % 2}"], tag="yev")
                bank_t = 6 + (hs % 2)
                sch.op("tensor", transpose8_fn(PBb(bank_t), yf[:], GTILES, idb[:]),
                       reads=[f"yfm{hs % 2}", "idb"], writes=[PTOK(bank_t)], tag="ytr")

                def yte(hs=hs, bank_t=bank_t):
                    dsl = y_tm[:, :, hs * 128:(hs + 1) * 128]
                    def f(s, g):
                        return s.copy(dsl, PBb(bank_t)[:, 0:GTILES * 128]
                                      .rearrange("p (j h) -> p j h", j=GTILES))
                    return f
                sch.op("scalar", yte(), reads=[PTOK(bank_t)], writes=["ytm"], tag="yte")

        def stage4(grp):
            for tt in range(GTILES):
                tile = grp * GTILES + tt
                ob = outt[0]
                otok = f"{phase}ot0"
                stt = stats[2 + tt % 2]
                xb = xt4[tt % 2]
                xtok = f"{phase}xr{tt % 2}"
                sch.op("gpsimd", dma_fn(xb, xsrc[tile]), writes=[xtok], dma=True, tag="xl2")

                def sq_fn(tt=tt, ob=ob, stt=stt):
                    def f(s, g):
                        return s.activation(ob[:], y_tm[:, tt, :], AF.Square,
                                            accum_out=stt.vp[:])
                    return f
                sch.op("scalar", sq_fn(), reads=["ytm"], writes=[otok, stt.tok], tag="ysq")
                sch.op("vector", rms_stats_fn(stt.vp[:], stt, float(H), EPS_RMS),
                       reads=[stt.tok], writes=[stt.tok], small=True, tag="rms")

                def stt_fn(tt=tt, ob=ob, stt=stt, xb=xb):
                    def f(v, g):
                        ins = v.tensor_scalar(stt.hc[:], stt.r[:], 0.5, None, ALU.mult)
                        g(ins)
                        return v.scalar_tensor_tensor(ob[:], y_tm[:, tt, :], stt.hc[:],
                                                      xb, ALU.mult, ALU.add)
                    return f
                sch.op("vector", stt_fn(), reads=["ytm", stt.tok, xtok],
                       writes=[otok, stt.tok], tag="res")
                sch.op("gpsimd", dma_fn(xdst[tile], ob[:]), reads=[otok],
                       writes=[f"{phase}st{tile}"], dma=True, tag="xst")

        stage1a(0)
        stage1b(0)
        for grp in range(NGRP):
            stage2(grp, grp > 0)
            if grp + 1 < NGRP:
                stage1a(grp + 1)
                load2(0)
                load2(1)
            stage3(grp)
            if grp + 1 < NGRP:
                stage1b(grp + 1)
            stage4(grp)
        A.pop()

    if "F1" in phases:
        emit_ffn(x_in.rearrange("b (n p) h -> (b n) p h", p=128), x1s, "F1")
        sch.barrier()

    qkv_r = qkv.rearrange("(c p) o -> p c o", p=128)
    wo_r = wo.rearrange("(c p) o -> p c o", p=128)

    for sq_ in (range(SEQ) if "A" in phases else ()):
        A.push()
        grep = A.bf16(2048)
        brep = A.bf16(2048)
        xn_fm = A.bf16(16 * 2048).rearrange("p (c t) -> p c t", c=16)
        xt = A.f32(2048)
        v_t = A.bf16(2048)
        xntm = [A.bf16(2048) for _ in range(2)]
        qwf = [A.f32(2048).rearrange("p (c t) -> p c t", c=16) for _ in range(2)]
        qwb = [A.bf16(2048).rearrange("p (c t) -> p c t", c=16) for _ in range(2)]
        vwf = A.f32(16 * 512).rearrange("p (c t) -> p c t", c=16)
        vwb = A.bf16(16 * 512).rearrange("p (c t) -> p c t", c=16)
        qst = [A.bf16(512) for _ in range(2)]

        gbf = A.f32(2048)
        def gb_bcast(t_, src):
            sch.op("sync", dma_fn(gbf, bass.AP(src, 0, [[0, 128], [1, 2048]])),
                   reads=[f"gb{id(t_)}"], writes=["gbf"], dma=True, tag="gbl")
            sch.op("vector", copy_fn(t_, gbf), reads=["gbf"],
                   writes=[f"gb{id(t_)}"], tag="gbc")
        gb_bcast(grep, gamma)
        gb_bcast(brep, beta)

        def qk_load_cast(m):
            p = m % 2
            sch.op("sync", dma_fn(qwf[p][:, :, :], qkv_r[:, :, m * 128:(m + 1) * 128]),
                   reads=[f"qwb{p}_r"], writes=[f"qwf{p}"], dma=True, tag="qwl")
            sch.op("scalar", scopy_fn(qwb[p][:, :, :], qwf[p][:, :, :]),
                   reads=[f"qwf{p}"], writes=[f"qwb{p}"], tag="qwc")
        qk_load_cast(0)
        qk_load_cast(1)

        def ln2_stats_fn(xap, stt):
            def f(v, g):
                for c in range(4):
                    ins = v.bn_stats(stt.bns[:, 6 * c:6 * (c + 1)],
                                     xap[:, 512 * c:512 * (c + 1)])
                g(ins)
                ins = v.bn_aggr(stt.mv[:], stt.bns[:]); g(ins)
                ins = newton_rsqrt(v, g, stt, stt.mv[:, 1:2],
                                   1.0 + EPS_LN, EPS_LN * EPS_LN)
                g(ins)
                return v.tensor_scalar(stt.nmr[:], stt.mv[:, 0:1], stt.r[:], -1.0,
                                       ALU.mult, ALU.mult)
            return f

        for tl in range(16):
            tile = sq_ * 16 + tl
            sch.op("gpsimd", dma_fn(xt, x1s[tile]), writes=["Axt"], dma=True, tag="axl")
            sch.op("vector", ln2_stats_fn(xt, stats[0]),
                   reads=["Axt"], writes=[stats[0].tok], small=True, tag="ast1")
            sch.op("vector", apply_fn(v_t[:], xt, stats[0]),
                   reads=["Axt", stats[0].tok], writes=["Avt"], tag="aap1")

            def gbmul(p=tl % 2):
                def f(v, g):
                    ins = v.tensor_tensor(v_t[:], v_t[:], grep[:], ALU.mult)
                    g(ins)
                    return v.tensor_tensor(xntm[p][:], v_t[:], brep[:], ALU.add)
                return f
            sch.op("vector", gbmul(), reads=["Avt", f"gb{id(grep)}", f"gb{id(brep)}"],
                   writes=[f"Axn{tl % 2}", "Avt"], tag="agb")
            for half in range(2):
                bank = 6 + half
                sch.op("tensor", transpose8_fn(PBb(bank), xntm[tl % 2][:, half * 1024:], 8, idb[:]),
                       reads=[f"Axn{tl % 2}", "idb"], writes=[PTOK(bank)], tag="xtr")
                dsl = xn_fm[:, half * 8:half * 8 + 8, tl * 128:(tl + 1) * 128]
                sch.op("scalar", scopy_fn(dsl, PBb(bank)[:, 0:1024]),
                       reads=[PTOK(bank)], writes=[f"xnfm_q{tl // 4}"], tag="xte")

        if xnscr is not None:
            xn_dst = xnscr.rearrange("s c p t -> s p c t")[sq_]
            for tl in range(16):
                sch.op("gpsimd", dma_fn(xn_dst[:, :, tl * 128:(tl + 1) * 128],
                                        xn_fm[:, :, tl * 128:(tl + 1) * 128]),
                       reads=[f"xnfm_q{tl // 4}"], writes=[f"xnd{sq_}_{tl}"], dma=True, tag="xnd")

        for m in range(32):
            p = m % 2
            if m >= 2:
                qk_load_cast(m)
            dstt = qscr if m < 16 else kscr
            mm = m % 16
            for nb in range(4):
                bank = nb

                def qmm(p=p, nb=nb, bank=bank):
                    def f(t, g):
                        ins = None
                        for c in range(16):
                            ins = t.matmul(PB(bank), qwb[p][:, c, :],
                                           xn_fm[:, c, nb * 512:(nb + 1) * 512],
                                           start=(c == 0), stop=(c == 15))
                        return ins
                    return f
                sch.op("tensor", qmm(), reads=[f"qwb{p}", f"xnfm_q{nb}"], writes=[PTOK(bank)], tag="qmm")
                qp = qst[nb % 2]
                sch.op("scalar", scopy_fn(qp[:], PB(bank)), reads=[PTOK(bank)],
                       writes=[f"qst{nb % 2}"], tag="qev")
                sch.op("gpsimd", dma_fn(dstt[sq_, mm, :, nb * 512:(nb + 1) * 512], qp[:]),
                       reads=[f"qst{nb % 2}"], writes=[f"qs_{m}_{nb}"], dma=True, tag="qsw")
            sch.op("vector", lambda e, g: e.nop(), reads=[f"qwb{p}"],
                   writes=[f"qwb{p}_r"], small=True, tag="qwr")

        for fb in range(4):
            sch.op("sync", dma_fn(vwf[:, :, :], qkv_r[:, :, 4096 + fb * 512:4096 + (fb + 1) * 512]),
                   reads=["vwb_r"], writes=["vwf"], dma=True, tag="vwl")
            sch.op("scalar", scopy_fn(vwb[:, :, :], vwf[:, :, :]),
                   reads=["vwf"], writes=["vwb"], tag="vwc")
            for tl in range(16):
                bank = tl % 4

                def vmm(tl=tl, bank=bank):
                    def f(t, g):
                        ins = None
                        for c in range(16):
                            ins = t.matmul(PB(bank), xn_fm[:, c, tl * 128:(tl + 1) * 128],
                                           vwb[:, c, :], start=(c == 0), stop=(c == 15))
                        return ins
                    return f
                sch.op("tensor", vmm(), reads=["vwb", f"xnfm_q{tl // 4}"], writes=[PTOK(bank)], tag="vmm")
                qp = qst[tl % 2]
                sch.op("scalar", scopy_fn(qp[:], PB(bank)), reads=[PTOK(bank)],
                       writes=[f"qst{tl % 2}"], tag="vev")
                sch.op("gpsimd", dma_fn(vscr[sq_, tl * 128:(tl + 1) * 128, fb * 512:(fb + 1) * 512], qp[:]),
                       reads=[f"qst{tl % 2}"], writes=[f"vs_{fb}_{tl}"], dma=True, tag="vsw")
            sch.op("vector", lambda e, g: e.nop(), reads=["vwb"], writes=["vwb_r"],
                   small=True, tag="vwr")
        A.pop()
        sch.barrier()

        A.push()
        k_fm = A.bf16(16 * 2048).rearrange("p (c t) -> p c t", c=16)
        v_sb = A.bf16(16 * 2048).rearrange("p (c t) -> p c t", c=16)
        q_t = [A.bf16(16 * 128).rearrange("p (c t) -> p c t", c=16) for _ in range(2)]
        probs = [A.bf16(2048) for _ in range(2)]
        probsT = [A.bf16(16 * 128).rearrange("p (c t) -> p c t", c=16) for _ in range(2)]
        av_tm = [A.bf16(2048) for _ in range(2)]
        avst = [A.bf16(16 * 128).rearrange("p (c t) -> p c t", c=16) for _ in range(2)]

        kv_src = kscr.rearrange("s c p t -> s p c t")[sq_]
        vv_src = vscr.rearrange("s (c p) t -> s p c t", p=128)[sq_]
        for c4 in range(4):
            sch.op("sync", dma_fn(k_fm[:, 4 * c4:4 * (c4 + 1), :], kv_src[:, 4 * c4:4 * (c4 + 1), :]),
                   writes=["kfm"], dma=True, tag="kld")
            sch.op("sync", dma_fn(v_sb[:, 4 * c4:4 * (c4 + 1), :], vv_src[:, 4 * c4:4 * (c4 + 1), :]),
                   writes=["vsb"], dma=True, tag="vld")

        q_src = qscr.rearrange("s c p t -> s p c t")[sq_]
        for qt in range(16):
            p = qt % 2
            nk = qt + 1
            nkb = (nk * 128 + 511) // 512
            sch.op("sync", dma_fn(q_t[p][:, :, :], q_src[:, :, qt * 128:(qt + 1) * 128]),
                   reads=[f"qt{p}_r"], writes=[f"qt{p}"], dma=True, tag="qld")

            for kb in range(nkb):
                ncol = min(512, nk * 128 - kb * 512)

                def smm(p=p, kb=kb, ncol=ncol):
                    def f(t, g):
                        ins = None
                        for c in range(16):
                            ins = t.matmul(pall[:, kb * 512:kb * 512 + ncol],
                                           q_t[p][:, c, :],
                                           k_fm[:, c, kb * 512:kb * 512 + ncol],
                                           start=(c == 0), stop=(c == 15))
                        return ins
                    return f
                sch.op("tensor", smm(), reads=[f"qt{p}", "kfm"],
                       writes=[PTOK(kb)], tag="smm")
                if kb == nkb - 1:
                    def maskf(qt=qt, kb=kb):
                        def f(v, g):
                            return v.tensor_tensor(pall[:, qt * 128:(qt + 1) * 128],
                                                   pall[:, qt * 128:(qt + 1) * 128],
                                                   maskc[:], ALU.add)
                        return f
                    sch.op("vector", maskf(), reads=[PTOK(kb), "maskc"],
                           writes=[PTOK(kb)], small=True, tag="mask")

                def bmax(kb=kb, ncol=ncol):
                    def f(v, g):
                        return v.reduce_max(mxp[:, kb:kb + 1],
                                            pall[:, kb * 512:kb * 512 + ncol], AX)
                    return f
                sch.op("vector", bmax(), reads=[PTOK(kb)], writes=["mxp"],
                       small=True, tag="bmax")

            def nbf(p=p, nkb=nkb):
                def f(v, g):
                    ins = v.reduce_max(mxs[p][:], mxp[:, 0:nkb], AX)
                    g(ins)
                    return v.tensor_scalar(nbias[p][:], mxs[p][:], -QK_SCALE, None, ALU.mult)
                return f
            sch.op("vector", nbf(), reads=["mxp"], writes=[f"nb{p}", f"mx{p}"],
                   small=True, tag="nbias")

            def expf(p=p, nk=nk, qt=qt, sq_=sq_):
                def f(s, g):
                    return s.activation(probs[p][:, 0:nk * 128], pall[:, 0:nk * 128],
                                        AF.Exp, bias=nbias[p][:], scale=QK_SCALE,
                                        accum_out=rsum[:, sq_ * 16 + qt:sq_ * 16 + qt + 1])
                return f
            sch.op("scalar", expf(), reads=[PTOK(k) for k in range(nkb)] + [f"nb{p}"],
                   writes=[f"pr{p}", "rsum"], tag="exp")

            for half in range(0, nk, 8):
                hn = min(8, nk - half)
                bank = 6 + (half // 8)
                sch.op("tensor", transpose8_fn(PBb(bank), probs[p][:, half * 128:], hn, idb[:]),
                       reads=[f"pr{p}", "idb"], writes=[PTOK(bank)], tag="ptr")

                def pte(p=p, half=half, hn=hn, bank=bank):
                    dsl = probsT[p][:, half:half + hn, :]
                    def f(s, g):
                        return s.copy(dsl, PBb(bank)[:, 0:hn * 128]
                                      .rearrange("p (c t) -> p c t", t=128))
                    return f
                sch.op("scalar", pte(), reads=[PTOK(bank)], writes=[f"pT{p}"], tag="pte")

            for fb in range(4):
                bank = 4 + fb % 2

                def avmm(p=p, nk=nk, fb=fb, bank=bank):
                    def f(t, g):
                        ins = None
                        for kt in range(nk):
                            ins = t.matmul(PB(bank), probsT[p][:, kt, :],
                                           v_sb[:, kt, fb * 512:(fb + 1) * 512],
                                           start=(kt == 0), stop=(kt == nk - 1))
                        return ins
                    return f
                sch.op("tensor", avmm(), reads=[f"pT{p}", "vsb"], writes=[PTOK(bank)], tag="avmm")

                def avev(p=p, fb=fb, bank=bank):
                    def f(s, g):
                        return s.copy(av_tm[p][:, fb * 512:(fb + 1) * 512], PB(bank))
                    return f
                sch.op("scalar", avev(), reads=[PTOK(bank)], writes=[f"avt{p}"], tag="avev")

            for half in range(2):
                bank = 6 + half
                sch.op("tensor", transpose8_fn(PBb(bank), av_tm[p][:, half * 1024:], 8, idb[:]),
                       reads=[f"avt{p}", "idb"], writes=[PTOK(bank)], tag="avtr")

                def avte(p=p, half=half, bank=bank):
                    dsl = avst[p][:, half * 8:half * 8 + 8, :]
                    def f(s, g):
                        return s.copy(dsl, PBb(bank)[:, 0:1024]
                                      .rearrange("p (c t) -> p c t", t=128))
                    return f
                sch.op("scalar", avte(), reads=[PTOK(bank)], writes=[f"avs{p}"], tag="avte")
            av_dst = avscr.rearrange("s c p t -> s p c t")[sq_]
            sch.op("gpsimd", dma_fn(av_dst[:, :, qt * 128:(qt + 1) * 128], avst[p][:, :, :]),
                   reads=[f"avs{p}"], writes=[f"avw_{sq_}_{qt}"], dma=True, tag="avw")
            sch.op("vector", lambda e, g: e.nop(), reads=[f"qt{p}"], writes=[f"qt{p}_r"],
                   small=True, tag="qtr")
        A.pop()
        if rsumscr is not None:
            sch.op("gpsimd", dma_fn(rsumscr[sq_], rsum[:, :]), reads=["rsum"],
                   writes=[f"rsd{sq_}"], dma=True, tag="rsd")
        sch.barrier()

    A4_ON = "A4" in phases
    A.push()
    wo_sb = A.bf16(16 * 2048).rearrange("p (c t) -> p c t", c=16)
    wof = [A.f32(2048) for _ in range(2)]
    av_q = [A.bf16(16 * 128).rearrange("p (c t) -> p c t", c=16) for _ in range(2)]
    x1t = [A.f32(2048) for _ in range(2)]
    u4 = A.f32(2048)
    x1u = A.f32(2048)
    x2t = [A.f32(2048) for _ in range(2)]
    o2sb = [A.f32(512) for _ in range(2)]

    for c in (range(16) if A4_ON else ()):
        p = c % 2
        sch.op("sync", dma_fn(wof[p][:], wo_r[:, c, :]), reads=[f"wof{p}_r"],
               writes=[f"wof{p}"], dma=True, tag="wol")

        def woc(c=c, p=p):
            def f(v, g):
                return v.tensor_copy(wo_sb[:, c, :], wof[p][:])
            return f
        sch.op("vector", woc(), reads=[f"wof{p}"], writes=["wosb"], tag="woc")
        sch.op("vector", lambda e, g: e.nop(), reads=[f"wof{p}"], writes=[f"wof{p}_r"],
               small=True, tag="wor")

    av_src4 = avscr.rearrange("s c p t -> s p c t")
    for idx in (range(32) if A4_ON else ()):
        sq_, qt = idx // 16, idx % 16
        p = idx % 2
        sch.op("sync", dma_fn(av_q[p][:, :, :],
                              av_src4[sq_][:, :, qt * 128:(qt + 1) * 128]),
               reads=[f"avq{p}_r"], writes=[f"avq{p}"], dma=True, tag="avl")
        sch.op("gpsimd", dma_fn(x1t[p][:], x1s[idx]), reads=[f"x1t{p}_r"],
               writes=[f"x1t{p}"], dma=True, tag="x1l")
        stt = stats[p]
        sch.op("vector", ln_stats_fn(x1t[p], stt, EPS_LN),
               reads=[f"x1t{p}"], writes=[stt.tok], small=True, tag="a4st")
        sch.op("vector", apply_fn(u4[:], x1t[p], stt),
               reads=[f"x1t{p}", stt.tok], writes=["u4"], tag="a4ap")

        def addf(p=p):
            def f(e, g):
                return e.tensor_add(x1u[:], x1t[p][:], u4[:])
            return f
        sch.op("gpsimd", addf(), reads=[f"x1t{p}", "u4"], writes=["x1u"], tag="a4add")
        sch.op("vector", lambda e, g: e.nop(), reads=[f"x1t{p}"], writes=[f"x1t{p}_r"],
               small=True, tag="x1r")

        def rcf(p=p, idx=idx):
            def f(v, g):
                return v.reciprocal(rcps[p][:], rsum[:, idx:idx + 1])
            return f
        sch.op("vector", rcf(), reads=["rsum"], writes=[f"rcp{p}"], small=True, tag="rcp")

        for fb in range(4):
            bank = fb

            def omm(p=p, fb=fb, bank=bank):
                def f(t, g):
                    ins = None
                    for c in range(16):
                        ins = t.matmul(PB(bank), av_q[p][:, c, :],
                                       wo_sb[:, c, fb * 512:(fb + 1) * 512],
                                       start=(c == 0), stop=(c == 15))
                    return ins
                return f
            sch.op("tensor", omm(), reads=[f"avq{p}", "wosb"], writes=[PTOK(bank)], tag="omm")

            def oev(p=p, fb=fb, bank=bank):
                def f(s, g):
                    return s.activation(o2sb[fb % 2][:], PB(bank), AF.Identity,
                                        scale=rcps[p][:])
                return f
            sch.op("scalar", oev(), reads=[PTOK(bank), f"rcp{p}"],
                   writes=[f"o2sb{fb % 2}"], tag="oev")

            def epi(p=p, fb=fb):
                def f(v, g):
                    return v.tensor_add(x2t[p][:, fb * 512:(fb + 1) * 512],
                                        o2sb[fb % 2][:],
                                        x1u[:, fb * 512:(fb + 1) * 512])
                return f
            sch.op("vector", epi(), reads=[f"o2sb{fb % 2}", "x1u"],
                   writes=[f"x2t{p}"], tag="epi")
        sch.op("vector", lambda e, g: e.nop(), reads=[f"avq{p}"], writes=[f"avq{p}_r"],
               small=True, tag="avqr")
        sch.op("gpsimd", dma_fn(x2s[idx], x2t[p][:]), reads=[f"x2t{p}"],
               writes=[f"x2w{idx}"], dma=True, tag="x2w")
    A.pop()
    if A4_ON:
        sch.barrier()

    if "F2" in phases:
        emit_ffn(x2s, out.rearrange("b (n p) h -> (b n) p h", p=128), "F2")
    else:
        A.push()
        zt = A.f32(2048)
        sch.op("vector", lambda e, g: e.memset(zt, 0.0), writes=["zt"], tag="z")
        o3 = out.rearrange("b (n p) h -> (b n) p h", p=128)
        for tile in range(NT):
            sch.op("gpsimd", dma_fn(o3[tile], zt), reads=["zt"],
                   writes=[f"zo{tile}"], dma=True, tag="zo")
        A.pop()

    sch.finalize()

    sems = {}
    for eng in Sched.ENG:
        sems[(eng, False)] = en(nc.semaphore(f"se_{eng}"))
        sems[(eng, True)] = en(nc.semaphore(f"sd_{eng}"))
    hz = {eng: en(nc.semaphore(f"hz_{eng}")) for eng in ("vector", "scalar", "gpsimd")}
    hz_cnt = {eng: [0] for eng in hz}

    block = en(nc.Block())
    per_eng = {eng: [o for o in sch.ops if o["eng"] == eng] for eng in Sched.ENG}

    def make_body(eng):
        ops = per_eng[eng]
        def body(e):
            hw = {}
            if eng in hz:
                cnt = hz_cnt[eng]
                sem = hz[eng]
                def guard(ins):
                    cnt[0] += 1
                    ins.then_inc(sem, 1)
                    e.wait_ge(sem, cnt[0])
            else:
                def guard(ins):
                    pass
            for o in ops:
                for (peng, pdma, val) in sch.needed_waits(o):
                    key = (peng, pdma)
                    if hw.get(key, 0) < val:
                        e.wait_ge(sems[key], val)
                        hw[key] = val
                ins = o["fn"](e, guard)
                if o["inc"]:
                    ins.then_inc(sems[(eng, o["dma"])], 16 if o["dma"] else 1)
            for key, total in sch.final_counts.items():
                if hw.get(key, 0) < total:
                    e.wait_ge(sems[key], total)
        return body

    block.tensor(make_body("tensor"))
    block.vector(make_body("vector"))
    block.scalar(make_body("scalar"))
    block.sync(make_body("sync"))
    block.gpsimd(make_body("gpsimd"))

    st.close()
    return nc


_NC_CACHE = {}


def _get_program(dbg=(), phases=("P0", "F1", "A", "A4", "F2")):
    key = (tuple(sorted(dbg)), tuple(phases))
    if key not in _NC_CACHE:
        _NC_CACHE[key] = build_core_program(dbg, phases)
    return _NC_CACHE[key]


def kernel(L0_attn_x, L0_attn_qkv, L0_attn_o_proj, L0_attn_gamma, L0_attn_beta,
           L0_ffn_gate_up_packed, L0_ffn_gate_up_scales, L0_ffn_down_packed,
           L0_ffn_down_scales, _dbg=(), _run=True,
           _phases=("P0", "F1", "A", "A4", "F2")):
    x = np.ascontiguousarray(np.asarray(L0_attn_x, dtype=np.float32))
    qkv = np.ascontiguousarray(np.asarray(L0_attn_qkv, dtype=np.float32))
    wo = np.ascontiguousarray(np.asarray(L0_attn_o_proj, dtype=np.float32))
    gamma = np.ascontiguousarray(np.asarray(L0_attn_gamma, dtype=np.float32))
    beta = np.ascontiguousarray(np.asarray(L0_attn_beta, dtype=np.float32))
    gup = np.ascontiguousarray(np.asarray(L0_ffn_gate_up_packed, dtype=np.int32))
    gus = np.ascontiguousarray(np.asarray(L0_ffn_gate_up_scales, dtype=np.float16))
    dnp = np.ascontiguousarray(np.asarray(L0_ffn_down_packed, dtype=np.int32))
    dns = np.ascontiguousarray(np.asarray(L0_ffn_down_scales, dtype=np.float16))

    nc = _get_program(_dbg, _phases)
    in_maps = []
    for c in range(NCORES):
        in_maps.append({
            "x": x[SEQ * c:SEQ * (c + 1)],
            "qkv": qkv, "wo": wo, "gamma": gamma, "beta": beta,
            "gup": np.ascontiguousarray(gup[:, 32 * c:32 * (c + 1)]),
            "gus": np.ascontiguousarray(gus[2 * c:2 * (c + 1), :]),
            "dnp": np.ascontiguousarray(dnp[256 * c:256 * (c + 1), :]),
            "dns": np.ascontiguousarray(dns[:, 256 * c:256 * (c + 1)]),
        })
    if not _run:
        return nc, in_maps
    res = run_bass_kernel_spmd(nc, in_maps, core_ids=list(range(NCORES)))
    outs = np.concatenate([r["out"] for r in res.results], axis=0)
    if _dbg:
        return outs.reshape(16, 2048, 2048), res
    return outs.reshape(16, 2048, 2048)


# version 25
# speedup vs baseline: 1.0242x; 1.0008x over previous
import contextlib
import numpy as np

import concourse.bass as bass
import concourse.mybir as mybir
from concourse.bass_utils import run_bass_kernel_spmd

F32 = mybir.dt.float32
BF16 = mybir.dt.bfloat16
U16 = mybir.dt.uint16
I32 = mybir.dt.int32
F16 = mybir.dt.float16
AF = mybir.ActivationFunctionType
ALU = mybir.AluOpType
AX = mybir.AxisListType.X

NCORES = 8
SEQ = 2
S = 2048
H = 2048
I = 5632
I2 = 2 * I
T = SEQ * S
NT = T // 128
GTILES = 4
NGRP = NT // GTILES
GUPW = H // 8
DNPW = I // 8
NEWTON = 2
QK_SCALE = 1.0 / float(np.sqrt(2048.0))
EPS_LN = 1e-5
EPS_RMS = 1e-6
MAGIC = 0x5F3759DF



SERIAL = False
FULL_EDGES = False
FULL_EDGE_ENGINES = ("vector", "gpsimd")


class Sched:
    ENG = ("tensor", "vector", "scalar", "sync", "gpsimd")

    def __init__(self):
        self.ops = []
        self.last_w = {}
        self.readers = {}
        self.barrier_op = None

    def op(self, eng, fn, reads=(), writes=(), dma=False, small=False, tag=""):
        i = len(self.ops)
        deps = set()
        for t in reads:
            j = self.last_w.get(t)
            if j is not None:
                deps.add(j)
        for t in writes:
            j = self.last_w.get(t)
            if j is not None:
                deps.add(j)
            for j2 in self.readers.get(t, ()):
                deps.add(j2)
        if self.barrier_op is not None:
            deps.add(self.barrier_op)
        if SERIAL and i > 0:
            deps.add(i - 1)
        deps.discard(i)
        o = dict(i=i, eng=eng, fn=fn, deps=sorted(deps), dma=dma, small=small,
                 inc=bool(dma), val=None, tag=tag)
        self.ops.append(o)
        for t in reads:
            self.readers.setdefault(t, []).append(i)
        for t in writes:
            self.last_w[t] = i
            self.readers[t] = []
        return i

    def barrier(self):
        last = {}
        for o in self.ops:
            last[o["eng"]] = o["i"]
        deps = sorted(last.values())
        i = len(self.ops)
        o = dict(i=i, eng="vector", fn=lambda e, g: e.nop(), deps=deps,
                 dma=False, small=False, inc=False, val=None, tag="barrier")
        self.ops.append(o)
        self.barrier_op = i

    def finalize(self):
        for o in self.ops:
            for j in o["deps"]:
                p = self.ops[j]
                if p["inc"]:
                    continue
                if p["dma"] or p["eng"] != o["eng"]:
                    p["inc"] = True
                elif p["eng"] in ("vector", "gpsimd") and p["small"]:
                    p["inc"] = True
        cnt = {}
        for o in self.ops:
            if o["inc"]:
                key = (o["eng"], o["dma"])
                step = 16 if o["dma"] else 1
                cnt[key] = cnt.get(key, 0) + step
                o["val"] = cnt[key]
        self.final_counts = cnt

    def needed_waits(self, o):
        ws = []
        for j in o["deps"]:
            p = self.ops[j]
            if not p["inc"]:
                continue
            same = p["eng"] == o["eng"]
            if same and not p["dma"] and not SERIAL and not FULL_EDGES \
                    and p["eng"] not in FULL_EDGE_ENGINES:
                if not (p["eng"] in ("vector", "gpsimd") and p["small"]):
                    continue
            ws.append((p["eng"], p["dma"], p["val"]))
        return ws


def build_core_program(dbg=(), phases=("P0", "F1", "A", "A4", "F2")):
    nc = bass.Bass("TRN2", target_bir_lowering=False)
    kind = lambda name: "ExternalOutput" if name in dbg else "Internal"

    x_in = nc.dram_tensor("x", [SEQ, S, H], F32, kind="ExternalInput")
    qkv = nc.dram_tensor("qkv", [H, 3 * H], F32, kind="ExternalInput")
    wo = nc.dram_tensor("wo", [H, H], F32, kind="ExternalInput")
    gamma = nc.dram_tensor("gamma", [H], F32, kind="ExternalInput")
    beta = nc.dram_tensor("beta", [H], F32, kind="ExternalInput")
    gup = nc.dram_tensor("gup", [I2, GUPW // 8], I32, kind="ExternalInput")
    gus = nc.dram_tensor("gus", [2, I2], F16, kind="ExternalInput")
    dnp = nc.dram_tensor("dnp", [H // 8, DNPW], I32, kind="ExternalInput")
    dns = nc.dram_tensor("dns", [44, H // 8], F16, kind="ExternalInput")
    out = nc.dram_tensor("out", [SEQ, S, H], F32, kind="ExternalOutput")

    wguT = nc.dram_tensor("wguT", [H, I2], BF16, kind="Internal")
    wdnT8 = nc.dram_tensor("wdnT8", [8, I, H // 8], BF16, kind="Internal")
    locg = nc.dram_tensor("locg", [H // 8, I2], BF16, kind="Internal")
    locd = nc.dram_tensor("locd", [I, H // 8], BF16, kind="Internal")
    x1s = nc.dram_tensor("x1s", [NT, 128, H], F32, kind=kind("x1s"))
    x2s = nc.dram_tensor("x2s", [NT, 128, H], F32, kind=kind("x2s"))
    qscr = nc.dram_tensor("qscr", [SEQ, 16, 128, S], BF16, kind=kind("qscr"))
    kscr = nc.dram_tensor("kscr", [SEQ, 16, 128, S], BF16, kind=kind("kscr"))
    vscr = nc.dram_tensor("vscr", [SEQ, S, H], BF16, kind=kind("vscr"))
    avscr = nc.dram_tensor("avscr", [SEQ, 16, 128, S], BF16, kind=kind("avscr"))
    xnscr = None
    if "xnscr" in dbg:
        xnscr = nc.dram_tensor("xnscr", [SEQ, 16, 128, S], BF16, kind="ExternalOutput")
    if "wgdbg" in dbg:
        wgdbg = nc.dram_tensor("wgdbg", [H, I2], BF16, kind="ExternalOutput")
        wddbg = nc.dram_tensor("wddbg", [8, I, H // 8], BF16, kind="ExternalOutput")
    rsumscr = None
    if "rsumscr" in dbg:
        rsumscr = nc.dram_tensor("rsumscr", [2, 128, 32], F32, kind="ExternalOutput")

    sch = Sched()
    st = contextlib.ExitStack()
    en = st.enter_context

    iot = en(nc.sbuf_tensor("iot", [128, 128], I32))
    idb = en(nc.sbuf_tensor("idb", [128, 128], BF16))
    idf = en(nc.sbuf_tensor("idf", [128, 128], F32))
    maskc = en(nc.sbuf_tensor("maskc", [128, 128], F32))
    rsum = en(nc.sbuf_tensor("rsum", [128, 32], F32))
    mxp = en(nc.sbuf_tensor("mxp", [128, 4], F32))
    sc16g = en(nc.sbuf_tensor("sc16g", [2, 512], F16))
    sc32g = en(nc.sbuf_tensor("sc32g", [2, 512], F32))
    sc16d = en(nc.sbuf_tensor("sc16d", [44, 128], F16))
    sc32d = en(nc.sbuf_tensor("sc32d", [44, 128], F32))
    scTg = [en(nc.sbuf_tensor(f"scTg{p}", [128, 16], F32)) for p in range(2)]
    scTd = [en(nc.sbuf_tensor(f"scTd{p}", [128, 44], F32)) for p in range(2)]

    class StatSet:
        def __init__(self, p):
            self.bns = en(nc.sbuf_tensor(f"bns{p}", [128, 24], F32))
            self.mv = en(nc.sbuf_tensor(f"mv{p}", [128, 2], F32))
            self.vp = en(nc.sbuf_tensor(f"vp{p}", [128, 1], F32))
            self.t1 = en(nc.sbuf_tensor(f"t1_{p}", [128, 1], I32))
            self.r = en(nc.sbuf_tensor(f"r{p}", [128, 1], F32))
            self.hc = en(nc.sbuf_tensor(f"hc{p}", [128, 1], F32))
            self.nmr = en(nc.sbuf_tensor(f"nmr{p}", [128, 1], F32))
            self.tok = f"stats{p}"
    stats = [StatSet(p) for p in range(4)]
    mxs = [en(nc.sbuf_tensor(f"mx{p}", [128, 1], F32)) for p in range(2)]
    nbias = [en(nc.sbuf_tensor(f"nb{p}", [128, 1], F32)) for p in range(2)]
    rcps = [en(nc.sbuf_tensor(f"rcp{p}", [128, 1], F32)) for p in range(2)]

    pall = en(nc.psum_tensor("pall", [128, 4096], F32))
    PB = lambda k: pall[:, 512 * k:512 * (k + 1)]
    PBb = lambda k: pall[:, 512 * k:512 * (k + 1)].bitcast(BF16)
    PTOK = lambda k: f"P{k}"

    ACOLS = 51264
    arena_t = en(nc.sbuf_tensor("arena", [128, ACOLS], F32))

    class Arena:
        def __init__(self):
            self.base = 0
            self.stack = []
        def push(self):
            self.stack.append(self.base)
        def pop(self):
            self.base = self.stack.pop()
        def f32(self, n):
            o = self.base
            self.base += n
            assert self.base <= ACOLS, f"arena overflow {self.base}"
            return arena_t[:, o:o + n]
        def bf16(self, n):
            assert n % 2 == 0
            return self.f32(n // 2).bitcast(BF16)
        def u16(self, n):
            assert n % 2 == 0
            return self.f32(n // 2).bitcast(U16)
        def i32(self, n):
            return self.f32(n).bitcast(I32)
    A = Arena()

    def f_iota(e, g):
        return e.iota(iot[:], [[-1, 128]], base=0, channel_multiplier=1)
    sch.op("gpsimd", f_iota, writes=["iot"], tag="iota")

    def f_ids(e, g):
        e.tensor_scalar(idf[:], iot[:], 0, 1.0, ALU.is_equal, ALU.mult)
        e.tensor_scalar(idb[:], iot[:], 0, 1.0, ALU.is_equal, ALU.mult)
        return e.tensor_scalar(maskc[:], iot[:], 0, -1e30, ALU.is_lt, ALU.mult)
    sch.op("vector", f_ids, reads=["iot"], writes=["idf", "idb", "maskc"], tag="ids")

    def newton_rsqrt(v, g, stt, src_ap, scale, bias):
        ins = v.tensor_scalar(stt.vp[:], src_ap, scale, bias, ALU.mult, ALU.add)
        g(ins)
        ins = v.tensor_scalar(stt.t1[:], stt.vp[:].bitcast(I32), 1, None,
                              ALU.logical_shift_right)
        g(ins)
        ins = v.tensor_scalar(stt.t1[:], stt.t1[:], -1, MAGIC, ALU.mult, ALU.add)
        g(ins)
        for it in range(NEWTON):
            rin = stt.t1[:].bitcast(F32) if it == 0 else stt.r[:]
            ins = v.tensor_tensor(stt.hc[:], rin, rin, ALU.mult); g(ins)
            ins = v.tensor_tensor(stt.hc[:], stt.hc[:], stt.vp[:], ALU.mult); g(ins)
            ins = v.tensor_scalar(stt.hc[:], stt.hc[:], -0.5, 1.5, ALU.mult, ALU.add); g(ins)
            ins = v.tensor_tensor(stt.r[:], rin, stt.hc[:], ALU.mult)
            if it < NEWTON - 1:
                g(ins)
        return ins

    def ln_stats_fn(xap, stt, eps):
        def f(v, g):
            for c in range(4):
                ins = v.bn_stats(stt.bns[:, 6 * c:6 * (c + 1)], xap[:, 512 * c:512 * (c + 1)])
            g(ins)
            ins = v.bn_aggr(stt.mv[:], stt.bns[:]); g(ins)
            ins = newton_rsqrt(v, g, stt, stt.mv[:, 1:2], 1.0, eps)
            g(ins)
            return v.tensor_scalar(stt.nmr[:], stt.mv[:, 0:1], stt.r[:], -1.0,
                                   ALU.mult, ALU.mult)
        return f

    def rms_stats_fn(ssq_ap, stt, n, eps):
        def f(v, g):
            return newton_rsqrt(v, g, stt, ssq_ap, 1.0 / n, eps)
        return f

    def apply_fn(dst, src, stt):
        def f(v, g):
            return v.tensor_scalar(dst, src, stt.r[:], stt.nmr[:], ALU.mult, ALU.add)
        return f

    def dma_fn(e_dst, e_src):
        def f(e, g):
            return e.dma_start(e_dst, e_src)
        return f

    def transpose8_fn(pbank_bf, src, n, ident, width=128):
        def f(t, g):
            ins = None
            for j in range(n):
                ins = t.transpose(pbank_bf[:, j * width:(j + 1) * width],
                                  src[:, j * width:(j + 1) * width], ident)
            return ins
        return f

    def copy_fn(dst, src):
        def f(e, g):
            return e.tensor_copy(dst, src)
        return f

    def scopy_fn(dst, src):
        def f(e, g):
            return e.copy(dst, src)
        return f

    A.push()
    pk4 = [A.i32(8 * 32) for _ in range(2)]
    pkd = [A.i32(DNPW) for _ in range(2)]
    tsetV = (A.u16(2 * DNPW), A.u16(2 * DNPW), A.bf16(2 * DNPW),
             A.bf16(2 * DNPW), A.u16(2 * DNPW))
    valsg = A.bf16(8 * 256)
    svalg = [A.bf16(8 * 256) for _ in range(2)]
    valsd = A.bf16(I)
    svald = [A.bf16(I) for _ in range(2)]
    stagg = [A.bf16(2 * 1024) for _ in range(2)]
    stagd = A.bf16(44 * 256)

    def decode_fn(pk, nwords, vals_ap, ks, tset):
        nh = 2 * nwords
        pku = pk.bitcast(U16)
        va = vals_ap.bitcast(U16).rearrange("p (j k) -> p j k", k=4)
        ta, tg, tt2, tfm, tsb = tset

        def f(v, g):
            ins = None
            for k in ks:
                a = ta[:, :nh]
                if 4 * k <= 6:
                    v.tensor_scalar(a, pku, 6 - 4 * k, 0x01C0,
                                    ALU.logical_shift_left, ALU.bitwise_and)
                else:
                    v.tensor_scalar(a, pku, 4 * k - 6, 0x01C0,
                                    ALU.logical_shift_right, ALU.bitwise_and)
                gg = tg[:, :nh]
                v.tensor_scalar(gg, a, 0x3F00, None, ALU.add)
                gbf = gg.bitcast(BF16)
                v.tensor_scalar(tt2[:, :nh], gbf, 2.0, -1.0, ALU.mult, ALU.add)
                v.tensor_tensor(tfm[:, :nh], gbf, tt2[:, :nh], ALU.min)
                v.tensor_scalar(tsb[:, :nh], pku, 1 << (4 * k + 3), 12 - 4 * k,
                                ALU.bitwise_and, ALU.logical_shift_left)
                ins = v.tensor_tensor(va[:, :, k], tfm[:, :nh].bitcast(U16),
                                      tsb[:, :nh], ALU.bitwise_or)
            return ins
        return f

    def scale_fn(pairs, scT_ap):
        def f(v, g):
            ins = None
            for dst, s_ap, gi in pairs:
                ins = v.tensor_scalar(dst, s_ap, scT_ap[:, gi:gi + 1], None, ALU.mult)
            return ins
        return f

    def p0_emit():
        gup_r = gup.rearrange("(t p) w -> p t w", p=128)
        for g4 in range(11):
            pk = pk4[g4 % 2]
            pktok = f"pk{g4 % 2}"
            pk3 = pk.rearrange("p (t w) -> p t w", t=8)
            sch.op("sync", dma_fn(pk3, gup_r[:, 8 * g4:8 * (g4 + 1), :]),
                   writes=[pktok], dma=True, tag="pkl")
            for hh in range(2):
                sch.op("sync", dma_fn(sc16g[:, :],
                                      gus[:, 1024 * g4 + 512 * hh:1024 * g4 + 512 * (hh + 1)]),
                       reads=["sc32g"], writes=["sc16g"], dma=True, tag="scl")
                sch.op("vector", copy_fn(sc32g[:, :], sc16g[:, :]),
                       reads=["sc16g"], writes=["sc32g"], tag="scc")

                def sctr(hh=hh):
                    def f(t, g):
                        ins = None
                        for j in range(4):
                            ins = t.transpose(PB(6)[:, 8 * hh + 2 * j:8 * hh + 2 * j + 2],
                                              sc32g[:, 128 * j:128 * (j + 1)], idf[:2, :2])
                        return ins
                    return f
                sch.op("tensor", sctr(), reads=["sc32g", "idf"], writes=[PTOK(6)], tag="sctr")
                sch.op("scalar", scopy_fn(scTg[g4 % 2][:, 8 * hh:8 * (hh + 1)],
                                          PB(6)[:, 8 * hh:8 * (hh + 1)]),
                       reads=[PTOK(6)], writes=[f"scT{g4 % 2}"], tag="sce")
            sch.op("vector", decode_fn(pk, 8 * 32, valsg[:, :], (0, 1, 2, 3), tsetV),
                   reads=[pktok], writes=["valsV"], tag="dec")
            sva = svalg[g4 % 2]
            v3 = valsg.rearrange("p (t h) -> p t h", t=8)
            s3 = sva.rearrange("p (t h) -> p t h", t=8)
            pairs = [(s3[:, j, 128 * gi:128 * (gi + 1)], v3[:, j, 128 * gi:128 * (gi + 1)],
                      2 * j + gi) for j in range(8) for gi in range(2)]
            sch.op("vector", scale_fn(pairs, scTg[g4 % 2][:]),
                   reads=["valsV", f"scT{g4 % 2}"], writes=[f"sval{g4 % 2}"], tag="scale")
            stg = stagg[g4 % 2]
            stg3 = stg.rearrange("p (c o) -> p c o", c=2)
            for half in range(2):
                bank = 6 + half

                def wtr(half=half, bank=bank, s3=s3):
                    def f(t, g):
                        ins = None
                        for j in range(8):
                            ins = t.transpose(
                                PBb(bank)[:, 128 * j:128 * (j + 1)],
                                s3[:, j, 128 * half:128 * (half + 1)], idb[:])
                        return ins
                    return f
                sch.op("tensor", wtr(), reads=[f"sval{g4 % 2}", "idb"],
                       writes=[PTOK(bank)], tag="wtr")

                def wte(half=half, bank=bank, stg3=stg3):
                    def f(s, g):
                        return s.copy(stg3[:, half, :], PBb(bank)[:, 0:1024])
                    return f
                sch.op("scalar", wte(), reads=[PTOK(bank)],
                       writes=[f"stagg{g4 % 2}"], tag="wte")
            for c in range(2):
                dsl = locg[c * 128:(c + 1) * 128, 1024 * g4:1024 * (g4 + 1)]
                sch.op("gpsimd", dma_fn(dsl, stg3[:, c, :]), reads=[f"stagg{g4 % 2}"],
                       writes=[f"locg_{g4}_{c}"], dma=True, tag="wst")
        def ccg(e, g):
            return e.collective_compute(
                "AllGather", mybir.AluOpType.bypass,
                replica_groups=[list(range(NCORES))],
                ins=[locg.ap().opt()], outs=[wguT.ap().opt()])
        sch.op("gpsimd", ccg,
               reads=[f"locg_{g4}_{c}" for g4 in range(11) for c in range(2)],
               writes=["wguT_cc"], tag="ccg")

        dnp_r = dnp.rearrange("(t p) w -> p t w", p=128)
        stagd3 = stagd.rearrange("p (c o) -> p c o", c=44)
        for dt_ in range(2):
            pk = pkd[dt_ % 2]
            pktok = f"pkd{dt_ % 2}"
            sch.op("sync", dma_fn(pk[:, :], dnp_r[:, dt_, :]),
                   writes=[pktok], dma=True, tag="pkdl")
            sch.op("sync", dma_fn(sc16d[:, :], dns[:, 128 * dt_:128 * (dt_ + 1)]),
                   reads=["sc32d"], writes=["sc16d"], dma=True, tag="sdl")
            sch.op("vector", copy_fn(sc32d[:, :], sc16d[:, :]),
                   reads=["sc16d"], writes=["sc32d"], tag="sdc")

            def sdtr(dt_=dt_):
                def f(t, g):
                    return t.transpose(PB(6)[:, 0:44], sc32d[:, :], idf[:44, :44])
                return f
            sch.op("tensor", sdtr(), reads=["sc32d", "idf"], writes=[PTOK(6)], tag="sdtr")
            sch.op("scalar", scopy_fn(scTd[dt_ % 2][:, :], PB(6)[:, 0:44]),
                   reads=[PTOK(6)], writes=[f"scTd{dt_ % 2}"], tag="sde")
            sch.op("vector", decode_fn(pk[:, :], DNPW, valsd[:, :], (0, 1, 2, 3), tsetV),
                   reads=[pktok], writes=["valsV"], tag="ddec")
            sva = svald[dt_ % 2]
            pairs = [(sva[:, 128 * gi:128 * (gi + 1)], valsd[:, 128 * gi:128 * (gi + 1)], gi)
                     for gi in range(44)]
            sch.op("vector", scale_fn(pairs, scTd[dt_ % 2][:]),
                   reads=["valsV", f"scTd{dt_ % 2}"], writes=[f"svald{dt_ % 2}"], tag="dscale")
            for half in range(0, 44, 8):
                hn = min(8, 44 - half)
                bank = 6 + (half // 8) % 2
                sch.op("tensor", transpose8_fn(PBb(bank), sva[:, half * 128:], hn, idb[:]),
                       reads=[f"svald{dt_ % 2}", "idb"], writes=[PTOK(bank)], tag="dwtr")

                def dwte(half=half, hn=hn, bank=bank, dt_=dt_):
                    dsl = stagd3[:, half:half + hn, 128 * dt_:128 * (dt_ + 1)]
                    def f(s, g):
                        return s.copy(dsl, PBb(bank)[:, 0:hn * 128]
                                      .rearrange("p (c o) -> p c o", o=128))
                    return f
                sch.op("scalar", dwte(), reads=[PTOK(bank)], writes=["stagd"], tag="dwte")
        for c in range(44):
            dsl = locd[c * 128:(c + 1) * 128, :]
            sch.op("gpsimd", dma_fn(dsl, stagd3[:, c, :]), reads=["stagd"],
                   writes=[f"locd_{c}"], dma=True, tag="dst")
        def ccd(e, g):
            return e.collective_compute(
                "AllGather", mybir.AluOpType.bypass,
                replica_groups=[list(range(NCORES))],
                ins=[locd.ap().opt()],
                outs=[wdnT8.rearrange("r i o -> (r i) o").opt()])
        sch.op("gpsimd", ccd, reads=[f"locd_{c}" for c in range(44)],
               writes=["wdnT_cc"], tag="ccd")

    if "P0" in phases:
        p0_emit()
        if "wgdbg" in dbg:
            for c in range(16):
                sch.op("gpsimd", dma_fn(wgdbg[c * 128:(c + 1) * 128, :],
                                        wguT[c * 128:(c + 1) * 128, :]),
                       reads=["wguT_cc"], writes=[f"wgd{c}"], dma=True, tag="wgd")
            for rr in range(8):
                sch.op("gpsimd", dma_fn(wddbg[rr], wdnT8[rr]),
                       reads=["wdnT_cc"], writes=[f"wdd{rr}"], dma=True, tag="wdd")
        sch.barrier()
    A.pop()

    def emit_ffn(xsrc, xdst, phase):
        A.push()
        u_fm = A.bf16(16 * 512).rearrange("p (c t) -> p c t", c=16)
        h_act = A.bf16(44 * 512).rearrange("p (c t) -> p c t", c=44)
        wg = [A.bf16(16 * 256).rearrange("p (c t) -> p c t", c=16) for _ in range(2)]
        wu = [A.bf16(16 * 256).rearrange("p (c t) -> p c t", c=16) for _ in range(2)]
        wd = [A.bf16(44 * 128).rearrange("p (c t) -> p c t", c=44) for _ in range(3)]
        y_tm = A.bf16(GTILES * 2048).rearrange("p (j h) -> p j h", j=GTILES)
        yfm = [A.bf16(512)]
        xt2 = [A.f32(2048) for _ in range(2)]
        xt4 = [A.f32(2048) for _ in range(2)]
        outt = [A.f32(2048)]
        utm = [A.bf16(2048) for _ in range(4)]
        gsil = [A.bf16(512) for _ in range(2)]

        def stage1a(grp):
            for tt in range(GTILES):
                tile = grp * GTILES + tt
                xb = xt2[tt % 2]
                xtok = f"{phase}xt{tt % 2}"
                stt = stats[tt % 2]
                sch.op("gpsimd", dma_fn(xb, xsrc[tile]), writes=[xtok], dma=True, tag="xld")
                sch.op("vector", ln_stats_fn(xb, stt, EPS_LN),
                       reads=[xtok], writes=[stt.tok], small=True, tag="lnst")
                sch.op("vector", apply_fn(utm[tt][:], xb, stt),
                       reads=[xtok, stt.tok], writes=[f"{phase}utm{tt}"], tag="uap")

        def stage1b(grp):
            for tt in range(GTILES):
                for half in range(2):
                    bank = 0 + half
                    sch.op("tensor",
                           transpose8_fn(PBb(bank), utm[tt][:, half * 1024:], 8, idb[:]),
                           reads=[f"{phase}utm{tt}", "idb"], writes=[PTOK(bank)], tag="utr")
                    dsl = u_fm[:, half * 8:half * 8 + 8, tt * 128:(tt + 1) * 128]
                    sch.op("vector", copy_fn(dsl, PBb(bank)[:, 0:1024]),
                           reads=[PTOK(bank)], writes=["ufm"], tag="ute")

        def load2(b):
            wgb, wub = wg[b % 2], wu[b % 2]
            wgt, wut = f"wg{b % 2}", f"wu{b % 2}"
            src_g = wguT.rearrange("(c p) o -> p c o", p=128)[:, :, 256 * b:256 * (b + 1)]
            src_u = wguT.rearrange("(c p) o -> p c o", p=128)[:, :, I + 256 * b:I + 256 * (b + 1)]
            sch.op("sync", dma_fn(wgb, src_g), reads=["wguT_cc"], writes=[wgt], dma=True, tag="wgl")
            sch.op("sync", dma_fn(wub, src_u), reads=["wguT_cc"], writes=[wut], dma=True, tag="wul")

        def stage2(grp, preloaded):
            for b in range(22):
                wgb, wub = wg[b % 2], wu[b % 2]
                wgt, wut = f"wg{b % 2}", f"wu{b % 2}"
                if not (preloaded and b < 2):
                    load2(b)
                for s_ in range(2):
                    bg, bu = (0, 1) if s_ == 0 else (2, 3)
                    isl = 2 * b + s_

                    def mm32(bg=bg, bu=bu, wgb=wgb, wub=wub, s_=s_):
                        def f(t, g):
                            for c in range(16):
                                t.matmul(PB(bg), wgb[:, c, s_ * 128:(s_ + 1) * 128],
                                         u_fm[:, c, :], start=(c == 0), stop=(c == 15))
                            ins = None
                            for c in range(16):
                                ins = t.matmul(PB(bu), wub[:, c, s_ * 128:(s_ + 1) * 128],
                                               u_fm[:, c, :], start=(c == 0), stop=(c == 15))
                            return ins
                        return f
                    sch.op("tensor", mm32(), reads=[wgt, wut, "ufm"],
                           writes=[PTOK(bg), PTOK(bu)], tag="gumm")

                    def silu_fn(bg=bg, p=isl % 2):
                        def f(s, g):
                            return s.activation(gsil[p][:], PB(bg), AF.Silu)
                        return f
                    sch.op("scalar", silu_fn(), reads=[PTOK(bg)],
                           writes=[f"gs{isl % 2}"], tag="silu")

                    def hmul_fn(bu=bu, isl=isl, p=isl % 2):
                        def f(v, g):
                            return v.tensor_tensor(h_act[:, isl, :], PB(bu),
                                                   gsil[p][:], ALU.mult)
                        return f
                    sch.op("vector", hmul_fn(), reads=[PTOK(bu), f"gs{isl % 2}"],
                           writes=["hact"], tag="hmul")

        def stage3(grp):
            dsrc = wdnT8.rearrange("r (c p) o -> r p c o", p=128)
            for hs in range(16):
                wdb = wd[hs % 3]
                wdt = f"wd{hs % 3}"
                if hs == 0:
                    for hp in range(3):
                        src_p = dsrc[hp // 2][:, :, (hp % 2) * 128:(hp % 2) * 128 + 128]
                        sch.op("sync", dma_fn(wd[hp % 3], src_p), reads=["wdnT_cc"],
                               writes=[f"wd{hp % 3}"], dma=True, tag="wdl")
                bank_y = 4 + (hs % 2)

                def dmm(wdb=wdb, bank_y=bank_y):
                    def f(t, g):
                        ins = None
                        for c in range(44):
                            ins = t.matmul(PB(bank_y), wdb[:, c, :],
                                           h_act[:, c, :], start=(c == 0), stop=(c == 43))
                        return ins
                    return f
                sch.op("tensor", dmm(), reads=[wdt, "hact"],
                       writes=[PTOK(bank_y)], tag="dmm")
                if hs + 3 < 16:
                    hp = hs + 3
                    src_p = dsrc[hp // 2][:, :, (hp % 2) * 128:(hp % 2) * 128 + 128]
                    sch.op("sync", dma_fn(wd[hp % 3], src_p), reads=["wdnT_cc"],
                           writes=[f"wd{hp % 3}"], dma=True, tag="wdl")
                yf = yfm[0]
                sch.op("scalar", scopy_fn(yf[:], PB(bank_y)),
                       reads=[PTOK(bank_y)], writes=["yfm0"], tag="yev")
                bank_t = 6 + (hs % 2)
                sch.op("tensor", transpose8_fn(PBb(bank_t), yf[:], GTILES, idb[:]),
                       reads=["yfm0", "idb"], writes=[PTOK(bank_t)], tag="ytr")

                def yte(hs=hs, bank_t=bank_t):
                    dsl = y_tm[:, :, hs * 128:(hs + 1) * 128]
                    def f(s, g):
                        return s.copy(dsl, PBb(bank_t)[:, 0:GTILES * 128]
                                      .rearrange("p (j h) -> p j h", j=GTILES))
                    return f
                sch.op("scalar", yte(), reads=[PTOK(bank_t)], writes=["ytm"], tag="yte")

        def stage4(grp):
            for tt in range(GTILES):
                tile = grp * GTILES + tt
                ob = outt[0]
                otok = f"{phase}ot0"
                stt = stats[2 + tt % 2]
                xb = xt4[tt % 2]
                xtok = f"{phase}xr{tt % 2}"
                sch.op("gpsimd", dma_fn(xb, xsrc[tile]), writes=[xtok], dma=True, tag="xl2")

                def sq_fn(tt=tt, ob=ob, stt=stt):
                    def f(s, g):
                        return s.activation(ob[:], y_tm[:, tt, :], AF.Square,
                                            accum_out=stt.vp[:])
                    return f
                sch.op("scalar", sq_fn(), reads=["ytm"], writes=[otok, stt.tok], tag="ysq")
                sch.op("vector", rms_stats_fn(stt.vp[:], stt, float(H), EPS_RMS),
                       reads=[stt.tok], writes=[stt.tok], small=True, tag="rms")

                def stt_fn(tt=tt, ob=ob, stt=stt, xb=xb):
                    def f(v, g):
                        ins = v.tensor_scalar(stt.hc[:], stt.r[:], 0.5, None, ALU.mult)
                        g(ins)
                        return v.scalar_tensor_tensor(ob[:], y_tm[:, tt, :], stt.hc[:],
                                                      xb, ALU.mult, ALU.add)
                    return f
                sch.op("vector", stt_fn(), reads=["ytm", stt.tok, xtok],
                       writes=[otok, stt.tok], tag="res")
                sch.op("gpsimd", dma_fn(xdst[tile], ob[:]), reads=[otok],
                       writes=[f"{phase}st{tile}"], dma=True, tag="xst")

        stage1a(0)
        stage1b(0)
        for grp in range(NGRP):
            stage2(grp, grp > 0)
            if grp + 1 < NGRP:
                stage1a(grp + 1)
                load2(0)
                load2(1)
            stage3(grp)
            if grp + 1 < NGRP:
                stage1b(grp + 1)
            stage4(grp)
        A.pop()

    if "F1" in phases:
        emit_ffn(x_in.rearrange("b (n p) h -> (b n) p h", p=128), x1s, "F1")
        sch.barrier()

    qkv_r = qkv.rearrange("(c p) o -> p c o", p=128)
    wo_r = wo.rearrange("(c p) o -> p c o", p=128)

    for sq_ in (range(SEQ) if "A" in phases else ()):
        A.push()
        grep = A.bf16(2048)
        brep = A.bf16(2048)
        xn_fm = A.bf16(16 * 2048).rearrange("p (c t) -> p c t", c=16)
        xt = A.f32(2048)
        v_t = A.bf16(2048)
        xntm = [A.bf16(2048) for _ in range(2)]
        qwf = [A.f32(2048).rearrange("p (c t) -> p c t", c=16) for _ in range(2)]
        qwb = [A.bf16(2048).rearrange("p (c t) -> p c t", c=16) for _ in range(2)]
        vwf = A.f32(16 * 512).rearrange("p (c t) -> p c t", c=16)
        vwb = A.bf16(16 * 512).rearrange("p (c t) -> p c t", c=16)
        qst = [A.bf16(512) for _ in range(2)]

        gbf = A.f32(2048)
        def gb_bcast(t_, src):
            sch.op("sync", dma_fn(gbf, bass.AP(src, 0, [[0, 128], [1, 2048]])),
                   reads=[f"gb{id(t_)}"], writes=["gbf"], dma=True, tag="gbl")
            sch.op("vector", copy_fn(t_, gbf), reads=["gbf"],
                   writes=[f"gb{id(t_)}"], tag="gbc")
        gb_bcast(grep, gamma)
        gb_bcast(brep, beta)

        def qk_load_cast(m):
            p = m % 2
            sch.op("sync", dma_fn(qwf[p][:, :, :], qkv_r[:, :, m * 128:(m + 1) * 128]),
                   reads=[f"qwb{p}_r"], writes=[f"qwf{p}"], dma=True, tag="qwl")
            sch.op("scalar", scopy_fn(qwb[p][:, :, :], qwf[p][:, :, :]),
                   reads=[f"qwf{p}"], writes=[f"qwb{p}"], tag="qwc")
        qk_load_cast(0)
        qk_load_cast(1)

        def ln2_stats_fn(xap, stt):
            def f(v, g):
                for c in range(4):
                    ins = v.bn_stats(stt.bns[:, 6 * c:6 * (c + 1)],
                                     xap[:, 512 * c:512 * (c + 1)])
                g(ins)
                ins = v.bn_aggr(stt.mv[:], stt.bns[:]); g(ins)
                ins = newton_rsqrt(v, g, stt, stt.mv[:, 1:2],
                                   1.0 + EPS_LN, EPS_LN * EPS_LN)
                g(ins)
                return v.tensor_scalar(stt.nmr[:], stt.mv[:, 0:1], stt.r[:], -1.0,
                                       ALU.mult, ALU.mult)
            return f

        for tl in range(16):
            tile = sq_ * 16 + tl
            sch.op("gpsimd", dma_fn(xt, x1s[tile]), writes=["Axt"], dma=True, tag="axl")
            sch.op("vector", ln2_stats_fn(xt, stats[0]),
                   reads=["Axt"], writes=[stats[0].tok], small=True, tag="ast1")
            sch.op("vector", apply_fn(v_t[:], xt, stats[0]),
                   reads=["Axt", stats[0].tok], writes=["Avt"], tag="aap1")

            def gbmul(p=tl % 2):
                def f(v, g):
                    ins = v.tensor_tensor(v_t[:], v_t[:], grep[:], ALU.mult)
                    g(ins)
                    return v.tensor_tensor(xntm[p][:], v_t[:], brep[:], ALU.add)
                return f
            sch.op("vector", gbmul(), reads=["Avt", f"gb{id(grep)}", f"gb{id(brep)}"],
                   writes=[f"Axn{tl % 2}", "Avt"], tag="agb")
            for half in range(2):
                bank = 6 + half
                sch.op("tensor", transpose8_fn(PBb(bank), xntm[tl % 2][:, half * 1024:], 8, idb[:]),
                       reads=[f"Axn{tl % 2}", "idb"], writes=[PTOK(bank)], tag="xtr")
                dsl = xn_fm[:, half * 8:half * 8 + 8, tl * 128:(tl + 1) * 128]
                sch.op("scalar", scopy_fn(dsl, PBb(bank)[:, 0:1024]),
                       reads=[PTOK(bank)], writes=[f"xnfm_q{tl // 4}"], tag="xte")

        if xnscr is not None:
            xn_dst = xnscr.rearrange("s c p t -> s p c t")[sq_]
            for tl in range(16):
                sch.op("gpsimd", dma_fn(xn_dst[:, :, tl * 128:(tl + 1) * 128],
                                        xn_fm[:, :, tl * 128:(tl + 1) * 128]),
                       reads=[f"xnfm_q{tl // 4}"], writes=[f"xnd{sq_}_{tl}"], dma=True, tag="xnd")

        for m in range(32):
            p = m % 2
            if m >= 2:
                qk_load_cast(m)
            dstt = qscr if m < 16 else kscr
            mm = m % 16
            for nb in range(4):
                bank = nb

                def qmm(p=p, nb=nb, bank=bank):
                    def f(t, g):
                        ins = None
                        for c in range(16):
                            ins = t.matmul(PB(bank), qwb[p][:, c, :],
                                           xn_fm[:, c, nb * 512:(nb + 1) * 512],
                                           start=(c == 0), stop=(c == 15))
                        return ins
                    return f
                sch.op("tensor", qmm(), reads=[f"qwb{p}", f"xnfm_q{nb}"], writes=[PTOK(bank)], tag="qmm")
                qp = qst[nb % 2]
                sch.op("scalar", scopy_fn(qp[:], PB(bank)), reads=[PTOK(bank)],
                       writes=[f"qst{nb % 2}"], tag="qev")
                sch.op("gpsimd", dma_fn(dstt[sq_, mm, :, nb * 512:(nb + 1) * 512], qp[:]),
                       reads=[f"qst{nb % 2}"], writes=[f"qs_{m}_{nb}"], dma=True, tag="qsw")
            sch.op("vector", lambda e, g: e.nop(), reads=[f"qwb{p}"],
                   writes=[f"qwb{p}_r"], small=True, tag="qwr")

        for fb in range(4):
            sch.op("sync", dma_fn(vwf[:, :, :], qkv_r[:, :, 4096 + fb * 512:4096 + (fb + 1) * 512]),
                   reads=["vwb_r"], writes=["vwf"], dma=True, tag="vwl")
            sch.op("scalar", scopy_fn(vwb[:, :, :], vwf[:, :, :]),
                   reads=["vwf"], writes=["vwb"], tag="vwc")
            for tl in range(16):
                bank = tl % 4

                def vmm(tl=tl, bank=bank):
                    def f(t, g):
                        ins = None
                        for c in range(16):
                            ins = t.matmul(PB(bank), xn_fm[:, c, tl * 128:(tl + 1) * 128],
                                           vwb[:, c, :], start=(c == 0), stop=(c == 15))
                        return ins
                    return f
                sch.op("tensor", vmm(), reads=["vwb", f"xnfm_q{tl // 4}"], writes=[PTOK(bank)], tag="vmm")
                qp = qst[tl % 2]
                sch.op("scalar", scopy_fn(qp[:], PB(bank)), reads=[PTOK(bank)],
                       writes=[f"qst{tl % 2}"], tag="vev")
                sch.op("gpsimd", dma_fn(vscr[sq_, tl * 128:(tl + 1) * 128, fb * 512:(fb + 1) * 512], qp[:]),
                       reads=[f"qst{tl % 2}"], writes=[f"vs_{fb}_{tl}"], dma=True, tag="vsw")
            sch.op("vector", lambda e, g: e.nop(), reads=["vwb"], writes=["vwb_r"],
                   small=True, tag="vwr")
        A.pop()
        sch.barrier()

        A.push()
        k_fm = A.bf16(16 * 2048).rearrange("p (c t) -> p c t", c=16)
        v_sb = A.bf16(16 * 2048).rearrange("p (c t) -> p c t", c=16)
        q_t = [A.bf16(16 * 128).rearrange("p (c t) -> p c t", c=16) for _ in range(2)]
        probs = [A.bf16(2048) for _ in range(2)]
        probsT = [A.bf16(16 * 128).rearrange("p (c t) -> p c t", c=16) for _ in range(2)]
        av_tm = [A.bf16(2048) for _ in range(2)]
        avst = [A.bf16(16 * 128).rearrange("p (c t) -> p c t", c=16) for _ in range(2)]

        kv_src = kscr.rearrange("s c p t -> s p c t")[sq_]
        vv_src = vscr.rearrange("s (c p) t -> s p c t", p=128)[sq_]
        for c4 in range(4):
            sch.op("sync", dma_fn(k_fm[:, 4 * c4:4 * (c4 + 1), :], kv_src[:, 4 * c4:4 * (c4 + 1), :]),
                   writes=["kfm"], dma=True, tag="kld")
            sch.op("sync", dma_fn(v_sb[:, 4 * c4:4 * (c4 + 1), :], vv_src[:, 4 * c4:4 * (c4 + 1), :]),
                   writes=["vsb"], dma=True, tag="vld")

        q_src = qscr.rearrange("s c p t -> s p c t")[sq_]
        for qt in range(16):
            p = qt % 2
            nk = qt + 1
            nkb = (nk * 128 + 511) // 512
            sch.op("sync", dma_fn(q_t[p][:, :, :], q_src[:, :, qt * 128:(qt + 1) * 128]),
                   reads=[f"qt{p}_r"], writes=[f"qt{p}"], dma=True, tag="qld")

            for kb in range(nkb):
                ncol = min(512, nk * 128 - kb * 512)

                def smm(p=p, kb=kb, ncol=ncol):
                    def f(t, g):
                        ins = None
                        for c in range(16):
                            ins = t.matmul(pall[:, kb * 512:kb * 512 + ncol],
                                           q_t[p][:, c, :],
                                           k_fm[:, c, kb * 512:kb * 512 + ncol],
                                           start=(c == 0), stop=(c == 15))
                        return ins
                    return f
                sch.op("tensor", smm(), reads=[f"qt{p}", "kfm"],
                       writes=[PTOK(kb)], tag="smm")
                if kb == nkb - 1:
                    def maskf(qt=qt, kb=kb):
                        def f(v, g):
                            return v.tensor_tensor(pall[:, qt * 128:(qt + 1) * 128],
                                                   pall[:, qt * 128:(qt + 1) * 128],
                                                   maskc[:], ALU.add)
                        return f
                    sch.op("vector", maskf(), reads=[PTOK(kb), "maskc"],
                           writes=[PTOK(kb)], small=True, tag="mask")

                def bmax(kb=kb, ncol=ncol):
                    def f(v, g):
                        return v.reduce_max(mxp[:, kb:kb + 1],
                                            pall[:, kb * 512:kb * 512 + ncol], AX)
                    return f
                sch.op("vector", bmax(), reads=[PTOK(kb)], writes=["mxp"],
                       small=True, tag="bmax")

            def nbf(p=p, nkb=nkb):
                def f(v, g):
                    ins = v.reduce_max(mxs[p][:], mxp[:, 0:nkb], AX)
                    g(ins)
                    return v.tensor_scalar(nbias[p][:], mxs[p][:], -QK_SCALE, None, ALU.mult)
                return f
            sch.op("vector", nbf(), reads=["mxp"], writes=[f"nb{p}", f"mx{p}"],
                   small=True, tag="nbias")

            def expf(p=p, nk=nk, qt=qt, sq_=sq_):
                def f(s, g):
                    return s.activation(probs[p][:, 0:nk * 128], pall[:, 0:nk * 128],
                                        AF.Exp, bias=nbias[p][:], scale=QK_SCALE,
                                        accum_out=rsum[:, sq_ * 16 + qt:sq_ * 16 + qt + 1])
                return f
            sch.op("scalar", expf(), reads=[PTOK(k) for k in range(nkb)] + [f"nb{p}"],
                   writes=[f"pr{p}", "rsum"], tag="exp")

            for half in range(0, nk, 8):
                hn = min(8, nk - half)
                bank = 6 + (half // 8)
                sch.op("tensor", transpose8_fn(PBb(bank), probs[p][:, half * 128:], hn, idb[:]),
                       reads=[f"pr{p}", "idb"], writes=[PTOK(bank)], tag="ptr")

                def pte(p=p, half=half, hn=hn, bank=bank):
                    dsl = probsT[p][:, half:half + hn, :]
                    def f(s, g):
                        return s.copy(dsl, PBb(bank)[:, 0:hn * 128]
                                      .rearrange("p (c t) -> p c t", t=128))
                    return f
                sch.op("scalar", pte(), reads=[PTOK(bank)], writes=[f"pT{p}"], tag="pte")

            for fb in range(4):
                bank = 4 + fb % 2

                def avmm(p=p, nk=nk, fb=fb, bank=bank):
                    def f(t, g):
                        ins = None
                        for kt in range(nk):
                            ins = t.matmul(PB(bank), probsT[p][:, kt, :],
                                           v_sb[:, kt, fb * 512:(fb + 1) * 512],
                                           start=(kt == 0), stop=(kt == nk - 1))
                        return ins
                    return f
                sch.op("tensor", avmm(), reads=[f"pT{p}", "vsb"], writes=[PTOK(bank)], tag="avmm")

                def avev(p=p, fb=fb, bank=bank):
                    def f(s, g):
                        return s.copy(av_tm[p][:, fb * 512:(fb + 1) * 512], PB(bank))
                    return f
                sch.op("scalar", avev(), reads=[PTOK(bank)], writes=[f"avt{p}"], tag="avev")

            for half in range(2):
                bank = 6 + half
                sch.op("tensor", transpose8_fn(PBb(bank), av_tm[p][:, half * 1024:], 8, idb[:]),
                       reads=[f"avt{p}", "idb"], writes=[PTOK(bank)], tag="avtr")

                def avte(p=p, half=half, bank=bank):
                    dsl = avst[p][:, half * 8:half * 8 + 8, :]
                    def f(s, g):
                        return s.copy(dsl, PBb(bank)[:, 0:1024]
                                      .rearrange("p (c t) -> p c t", t=128))
                    return f
                sch.op("scalar", avte(), reads=[PTOK(bank)], writes=[f"avs{p}"], tag="avte")
            av_dst = avscr.rearrange("s c p t -> s p c t")[sq_]
            sch.op("gpsimd", dma_fn(av_dst[:, :, qt * 128:(qt + 1) * 128], avst[p][:, :, :]),
                   reads=[f"avs{p}"], writes=[f"avw_{sq_}_{qt}"], dma=True, tag="avw")
            sch.op("vector", lambda e, g: e.nop(), reads=[f"qt{p}"], writes=[f"qt{p}_r"],
                   small=True, tag="qtr")
        A.pop()
        if rsumscr is not None:
            sch.op("gpsimd", dma_fn(rsumscr[sq_], rsum[:, :]), reads=["rsum"],
                   writes=[f"rsd{sq_}"], dma=True, tag="rsd")
        sch.barrier()

    A4_ON = "A4" in phases
    A.push()
    wo_sb = A.bf16(16 * 2048).rearrange("p (c t) -> p c t", c=16)
    wof = [A.f32(2048) for _ in range(2)]
    av_q = [A.bf16(16 * 128).rearrange("p (c t) -> p c t", c=16) for _ in range(2)]
    x1t = [A.f32(2048) for _ in range(2)]
    u4 = A.f32(2048)
    x1u = A.f32(2048)
    x2t = [A.f32(2048) for _ in range(2)]
    o2sb = [A.f32(512) for _ in range(2)]

    for c in (range(16) if A4_ON else ()):
        p = c % 2
        sch.op("sync", dma_fn(wof[p][:], wo_r[:, c, :]), reads=[f"wof{p}_r"],
               writes=[f"wof{p}"], dma=True, tag="wol")

        def woc(c=c, p=p):
            def f(v, g):
                return v.tensor_copy(wo_sb[:, c, :], wof[p][:])
            return f
        sch.op("vector", woc(), reads=[f"wof{p}"], writes=["wosb"], tag="woc")
        sch.op("vector", lambda e, g: e.nop(), reads=[f"wof{p}"], writes=[f"wof{p}_r"],
               small=True, tag="wor")

    av_src4 = avscr.rearrange("s c p t -> s p c t")
    for idx in (range(32) if A4_ON else ()):
        sq_, qt = idx // 16, idx % 16
        p = idx % 2
        sch.op("sync", dma_fn(av_q[p][:, :, :],
                              av_src4[sq_][:, :, qt * 128:(qt + 1) * 128]),
               reads=[f"avq{p}_r"], writes=[f"avq{p}"], dma=True, tag="avl")
        sch.op("gpsimd", dma_fn(x1t[p][:], x1s[idx]), reads=[f"x1t{p}_r"],
               writes=[f"x1t{p}"], dma=True, tag="x1l")
        stt = stats[p]
        sch.op("vector", ln_stats_fn(x1t[p], stt, EPS_LN),
               reads=[f"x1t{p}"], writes=[stt.tok], small=True, tag="a4st")
        sch.op("vector", apply_fn(u4[:], x1t[p], stt),
               reads=[f"x1t{p}", stt.tok], writes=["u4"], tag="a4ap")

        def addf(p=p):
            def f(e, g):
                return e.tensor_add(x1u[:], x1t[p][:], u4[:])
            return f
        sch.op("gpsimd", addf(), reads=[f"x1t{p}", "u4"], writes=["x1u"], tag="a4add")
        sch.op("vector", lambda e, g: e.nop(), reads=[f"x1t{p}"], writes=[f"x1t{p}_r"],
               small=True, tag="x1r")

        def rcf(p=p, idx=idx):
            def f(v, g):
                return v.reciprocal(rcps[p][:], rsum[:, idx:idx + 1])
            return f
        sch.op("vector", rcf(), reads=["rsum"], writes=[f"rcp{p}"], small=True, tag="rcp")

        for fb in range(4):
            bank = fb

            def omm(p=p, fb=fb, bank=bank):
                def f(t, g):
                    ins = None
                    for c in range(16):
                        ins = t.matmul(PB(bank), av_q[p][:, c, :],
                                       wo_sb[:, c, fb * 512:(fb + 1) * 512],
                                       start=(c == 0), stop=(c == 15))
                    return ins
                return f
            sch.op("tensor", omm(), reads=[f"avq{p}", "wosb"], writes=[PTOK(bank)], tag="omm")

            def oev(p=p, fb=fb, bank=bank):
                def f(s, g):
                    return s.activation(o2sb[fb % 2][:], PB(bank), AF.Identity,
                                        scale=rcps[p][:])
                return f
            sch.op("scalar", oev(), reads=[PTOK(bank), f"rcp{p}"],
                   writes=[f"o2sb{fb % 2}"], tag="oev")

            def epi(p=p, fb=fb):
                def f(v, g):
                    return v.tensor_add(x2t[p][:, fb * 512:(fb + 1) * 512],
                                        o2sb[fb % 2][:],
                                        x1u[:, fb * 512:(fb + 1) * 512])
                return f
            sch.op("vector", epi(), reads=[f"o2sb{fb % 2}", "x1u"],
                   writes=[f"x2t{p}"], tag="epi")
        sch.op("vector", lambda e, g: e.nop(), reads=[f"avq{p}"], writes=[f"avq{p}_r"],
               small=True, tag="avqr")
        sch.op("gpsimd", dma_fn(x2s[idx], x2t[p][:]), reads=[f"x2t{p}"],
               writes=[f"x2w{idx}"], dma=True, tag="x2w")
    A.pop()
    if A4_ON:
        sch.barrier()

    if "F2" in phases:
        emit_ffn(x2s, out.rearrange("b (n p) h -> (b n) p h", p=128), "F2")
    else:
        A.push()
        zt = A.f32(2048)
        sch.op("vector", lambda e, g: e.memset(zt, 0.0), writes=["zt"], tag="z")
        o3 = out.rearrange("b (n p) h -> (b n) p h", p=128)
        for tile in range(NT):
            sch.op("gpsimd", dma_fn(o3[tile], zt), reads=["zt"],
                   writes=[f"zo{tile}"], dma=True, tag="zo")
        A.pop()

    sch.finalize()

    sems = {}
    for eng in Sched.ENG:
        sems[(eng, False)] = en(nc.semaphore(f"se_{eng}"))
        sems[(eng, True)] = en(nc.semaphore(f"sd_{eng}"))
    hz = {eng: en(nc.semaphore(f"hz_{eng}")) for eng in ("vector", "scalar", "gpsimd")}
    hz_cnt = {eng: [0] for eng in hz}

    block = en(nc.Block())
    per_eng = {eng: [o for o in sch.ops if o["eng"] == eng] for eng in Sched.ENG}

    def make_body(eng):
        ops = per_eng[eng]
        def body(e):
            hw = {}
            if eng in hz:
                cnt = hz_cnt[eng]
                sem = hz[eng]
                def guard(ins):
                    cnt[0] += 1
                    ins.then_inc(sem, 1)
                    e.wait_ge(sem, cnt[0])
            else:
                def guard(ins):
                    pass
            for o in ops:
                for (peng, pdma, val) in sch.needed_waits(o):
                    key = (peng, pdma)
                    if hw.get(key, 0) < val:
                        e.wait_ge(sems[key], val)
                        hw[key] = val
                ins = o["fn"](e, guard)
                if o["inc"]:
                    ins.then_inc(sems[(eng, o["dma"])], 16 if o["dma"] else 1)
            for key, total in sch.final_counts.items():
                if hw.get(key, 0) < total:
                    e.wait_ge(sems[key], total)
        return body

    block.tensor(make_body("tensor"))
    block.vector(make_body("vector"))
    block.scalar(make_body("scalar"))
    block.sync(make_body("sync"))
    block.gpsimd(make_body("gpsimd"))

    st.close()
    return nc


_NC_CACHE = {}


def _get_program(dbg=(), phases=("P0", "F1", "A", "A4", "F2")):
    key = (tuple(sorted(dbg)), tuple(phases))
    if key not in _NC_CACHE:
        _NC_CACHE[key] = build_core_program(dbg, phases)
    return _NC_CACHE[key]


def kernel(L0_attn_x, L0_attn_qkv, L0_attn_o_proj, L0_attn_gamma, L0_attn_beta,
           L0_ffn_gate_up_packed, L0_ffn_gate_up_scales, L0_ffn_down_packed,
           L0_ffn_down_scales, _dbg=(), _run=True,
           _phases=("P0", "F1", "A", "A4", "F2")):
    x = np.ascontiguousarray(np.asarray(L0_attn_x, dtype=np.float32))
    qkv = np.ascontiguousarray(np.asarray(L0_attn_qkv, dtype=np.float32))
    wo = np.ascontiguousarray(np.asarray(L0_attn_o_proj, dtype=np.float32))
    gamma = np.ascontiguousarray(np.asarray(L0_attn_gamma, dtype=np.float32))
    beta = np.ascontiguousarray(np.asarray(L0_attn_beta, dtype=np.float32))
    gup = np.ascontiguousarray(np.asarray(L0_ffn_gate_up_packed, dtype=np.int32))
    gus = np.ascontiguousarray(np.asarray(L0_ffn_gate_up_scales, dtype=np.float16))
    dnp = np.ascontiguousarray(np.asarray(L0_ffn_down_packed, dtype=np.int32))
    dns = np.ascontiguousarray(np.asarray(L0_ffn_down_scales, dtype=np.float16))

    nc = _get_program(_dbg, _phases)
    in_maps = []
    for c in range(NCORES):
        in_maps.append({
            "x": x[SEQ * c:SEQ * (c + 1)],
            "qkv": qkv, "wo": wo, "gamma": gamma, "beta": beta,
            "gup": np.ascontiguousarray(gup[:, 32 * c:32 * (c + 1)]),
            "gus": np.ascontiguousarray(gus[2 * c:2 * (c + 1), :]),
            "dnp": np.ascontiguousarray(dnp[256 * c:256 * (c + 1), :]),
            "dns": np.ascontiguousarray(dns[:, 256 * c:256 * (c + 1)]),
        })
    if not _run:
        return nc, in_maps
    res = run_bass_kernel_spmd(nc, in_maps, core_ids=list(range(NCORES)))
    outs = np.concatenate([r["out"] for r in res.results], axis=0)
    if _dbg:
        return outs.reshape(16, 2048, 2048), res
    return outs.reshape(16, 2048, 2048)
